# version 27
# baseline (speedup 1.0000x reference)
import contextlib
import numpy as np
import ml_dtypes

import concourse.bass as bass
import concourse.mybir as mybir
from concourse.alu_op_type import AluOpType as op
from concourse.bass_utils import run_bass_kernel_spmd

F32 = mybir.dt.float32
F16 = mybir.dt.float16
BF16 = mybir.dt.bfloat16
AF = mybir.ActivationFunctionType
AX = mybir.AxisListType.X

B, S, D, H, DK, I = 16, 2048, 1024, 16, 64, 4096
NCORE = 8
SPC = B // NCORE
TPC = SPC * S
ST = 512
NPASS = TPC // ST
NQ = S // 512


class KB:
    ENGINES = ("sync", "gpsimd", "tensor", "vector", "scalar")

    def __init__(self, nc):
        self.nc = nc
        self.streams = {e: [] for e in self.ENGINES}
        self.cnt = {}
        self.sems = {}

    def new_sem(self, es, name):
        self.sems[name] = es.enter_context(self.nc.semaphore(name))
        self.cnt[name] = 0

    SERIAL = {"vector": "V", "scalar": "Sc"}

    def emit(self, eng, fn, wait=(), inc=None, inc_by=1):
        sems = self.sems
        wait = tuple((s, v) for (s, v) in wait if v and v > 0)
        ser = self.SERIAL.get(eng)
        if ser is not None and inc is not None and inc != ser:
            ser = None
        elif ser is not None:
            assert inc is None or inc == ser
            inc, inc_by = ser, 1

        def run(e, fn=fn, wait=wait, inc=inc, inc_by=inc_by, ser=ser,
                k=(self.cnt[ser] if ser is not None else 0)):
            if ser is not None and k > 0:
                e.wait_ge(sems[ser], k)
            for s, v in wait:
                e.wait_ge(sems[s], v)
            r = fn(e)
            if inc is not None:
                r.then_inc(sems[inc], inc_by)

        self.streams[eng].append(run)
        if inc is not None:
            self.cnt[inc] += inc_by
            return self.cnt[inc]
        return None

    def flush(self, block):
        for eng in self.ENGINES:
            fns = self.streams[eng]
            if fns:
                getattr(block, eng)(lambda e, fns=tuple(fns): [f(e) for f in fns])
                self.streams[eng] = []

    def counts(self):
        return dict(self.cnt)


def emit_quant_apply(K, x_ap, rdq_ap, dq_ap, out_ap, tmp, wait=(), inc="V"):
    W = x_ap.shape[-1]
    nb = W // 16
    x3 = x_ap.rearrange("p (b k) -> p b k", k=16)
    rdq_b = rdq_ap.unsqueeze(2).broadcast_to([128, nb, 16])
    dq_b = dq_ap.unsqueeze(2).broadcast_to([128, nb, 16])
    A, Bt, C, Dt, E, Ft = (tmp[c] for c in "ABCDEF")
    A3 = A.rearrange("p (b k) -> p b k", k=16)
    v = K.emit
    sq = str(tmp["B"].dtype) == str(F32)
    v("vector", lambda e: e.tensor_tensor(A3, x3, rdq_b, op.mult), wait=wait)
    if sq:
        v("vector", lambda e: e.tensor_tensor(Bt, A, A, op.mult))
        thr = (0.0625, 0.5625, 1.5625, 3.0625, 6.25, 12.25, 25.0)
    else:
        v("vector", lambda e: e.tensor_single_scalar(Bt, A, -1.0, op.mult))
        v("vector", lambda e: e.tensor_tensor(Bt, A, Bt, op.max))
        thr = (0.25, 0.75, 1.25, 1.75, 2.5, 3.5, 5.0)
    v("vector", lambda e: e.tensor_scalar(C, A, 0.0, 2.0, op.is_ge, op.mult))
    v("vector", lambda e, t=thr[0]: e.tensor_single_scalar(Dt, Bt, t, op.is_ge))
    v("vector", lambda e, t=thr[1]: e.scalar_tensor_tensor(Dt, Bt, t, Dt, op.is_ge, op.add))
    v("vector", lambda e, t=thr[2]: e.scalar_tensor_tensor(Dt, Bt, t, Dt, op.is_ge, op.add))
    v("vector", lambda e, t=thr[3]: e.scalar_tensor_tensor(Dt, Bt, t, Dt, op.is_ge, op.add))
    v("vector", lambda e, t=thr[4]: e.tensor_single_scalar(E, Bt, t, op.is_ge))
    v("vector", lambda e, t=thr[5]: e.scalar_tensor_tensor(E, Bt, t, E, op.is_ge, op.add))
    v("vector", lambda e, t6=thr[6]: e.tensor_scalar(Ft, Bt, t6, 2.0, op.is_ge, op.mult))
    v("vector", lambda e: e.scalar_tensor_tensor(Dt, Dt, 0.5, E, op.mult, op.add))
    v("vector", lambda e: e.tensor_tensor(Dt, Dt, Ft, op.add))
    v("vector", lambda e, D3=Dt.rearrange("p (b k) -> p b k", k=16):
      e.tensor_tensor(D3, D3, dq_b, op.mult))
    return v("vector", lambda e:
             e.scalar_tensor_tensor(out_ap, C, 1.0, Dt, op.subtract, op.mult),
             inc=inc)


def emit_scale_pipeline(K, amax_ap, gs6_ap, gsinv_ap, dq_ap, rdq_ap, tmp,
                        wait=(), inc="V"):
    S0, T0, T1, T2, V0, G, GI = (tmp[c] for c in ("S0", "T0", "T1", "T2", "V0", "G", "GI"))
    v = K.emit
    v("vector", lambda e: e.tensor_scalar(S0, amax_ap, gs6_ap, 448.0, op.mult, op.min),
      wait=wait)
    v("vector", lambda e: e.tensor_single_scalar(S0, S0, 2.0 ** -6, op.max))
    v("vector", lambda e: e.memset(G, 2.0 ** -9))
    v("vector", lambda e: e.memset(GI, 2.0 ** 9))
    for k in (8, 4, 2, 1):
        thr = float(2.0 ** (k + 3))
        m2 = float(2.0 ** k - 1.0)
        m3 = float(2.0 ** -k - 1.0)
        v("vector", lambda e: e.tensor_tensor(T0, S0, GI, op.mult))
        v("vector", lambda e, thr=thr: e.tensor_single_scalar(T1, T0, thr, op.is_ge))
        v("vector", lambda e, m3=m3: e.tensor_scalar(T0, T1, m3, 1.0, op.mult, op.add))
        v("vector", lambda e: e.tensor_tensor(GI, GI, T0, op.mult))
        v("vector", lambda e, m2=m2: e.tensor_scalar(T0, T1, m2, 1.0, op.mult, op.add))
        v("vector", lambda e: e.tensor_tensor(G, G, T0, op.mult))
    v("vector", lambda e: e.tensor_tensor(V0, S0, GI, op.mult))
    v("vector", lambda e: e.tensor_single_scalar(T0, V0, 8.5, op.is_ge))
    for h in (9.5, 10.5, 11.5, 12.5, 13.5, 14.5, 15.5):
        v("vector", lambda e, h=h: e.scalar_tensor_tensor(T0, V0, h, T0,
                                                          op.is_ge, op.add))
    v("vector", lambda e: e.tensor_single_scalar(T0, T0, 8.0, op.add))
    v("vector", lambda e: e.tensor_tensor(S0, T0, G, op.mult))
    v("vector", lambda e: e.tensor_single_scalar(dq_ap, S0, gsinv_ap, op.mult))
    v("vector", lambda e: e.tensor_single_scalar(T0, dq_ap, 1e-3, op.max))
    v("vector", lambda e: e.reciprocal(T1, T0))
    return v("vector", lambda e: e.tensor_single_scalar(rdq_ap, T1, 1500.0, op.min),
             inc=inc)


def emit_ln_chunk(K, src_ap, out_ap, stat, sq_scratch, wait_load=(), inc="V",
                  apply_out=True):
    v = K.emit
    v("vector", lambda e: e.reduce_sum(stat[:, 0:1], src_ap, AX),
      wait=wait_load, inc="V")
    ssv = v("scalar", lambda e: e.activation(sq_scratch, src_ap, AF.Square,
                                             accum_out=stat[:, 1:2]),
            wait=wait_load, inc="Sc")

    v("vector", lambda e: e.tensor_single_scalar(stat[:, 0:1], stat[:, 0:1],
                                                 1.0 / 1024.0, op.mult),
      wait=[("Sc", ssv)])
    v("vector", lambda e: e.tensor_tensor(stat[:, 2:3], stat[:, 0:1],
                                          stat[:, 0:1], op.mult))
    v("vector", lambda e: e.scalar_tensor_tensor(stat[:, 2:3], stat[:, 1:2],
                                                 1.0 / 1024.0, stat[:, 2:3],
                                                 op.mult, op.subtract))
    v("vector", lambda e: e.tensor_single_scalar(stat[:, 2:3], stat[:, 2:3],
                                                 1e-5, op.add))
    rv = v("vector", lambda e: e.reciprocal(stat[:, 3:4], stat[:, 2:3]), inc="V")
    sqv = v("scalar", lambda e: e.activation(stat[:, 4:5], stat[:, 3:4], AF.Sqrt),
            wait=[("V", rv)], inc="Sc")
    v("vector", lambda e: e.tensor_tensor(stat[:, 5:6], stat[:, 4:5],
                                          stat[:, 4:5], op.mult),
      wait=[("Sc", sqv)])
    v("vector", lambda e: e.tensor_tensor(stat[:, 5:6], stat[:, 2:3],
                                          stat[:, 5:6], op.mult))
    v("vector", lambda e: e.tensor_scalar(stat[:, 5:6], stat[:, 5:6], -0.5, 1.5,
                                          op.mult, op.add))
    v("vector", lambda e: e.tensor_tensor(stat[:, 4:5], stat[:, 4:5],
                                          stat[:, 5:6], op.mult))
    if apply_out:
        return v("vector", lambda e: e.tensor_scalar(out_ap, src_ap, stat[:, 0:1],
                                                     stat[:, 4:5],
                                                     op.subtract, op.mult), inc=inc)
    v("vector", lambda e: e.tensor_copy(stat[:, 6:7], stat[:, 0:1]))
    return v("vector", lambda e: e.tensor_copy(stat[:, 7:8], stat[:, 4:5]), inc=inc)


def build_program():
    nc = bass.Bass("TRN2", target_bir_lowering=False, debug=False,
                   detect_race_conditions=False)
    es = contextlib.ExitStack()

    ap_x = nc.dram_tensor("x", [SPC, S, D], F32, kind="ExternalInput").ap()
    ap_qT = nc.dram_tensor("qT", [8, 128, 1024], F32, kind="ExternalInput").ap()
    ap_kT = nc.dram_tensor("kT", [8, 128, 1024], F32, kind="ExternalInput").ap()
    ap_vT = nc.dram_tensor("vT", [D, D], F32, kind="ExternalInput").ap()
    ap_oT = nc.dram_tensor("oT", [D, D], F32, kind="ExternalInput").ap()
    ap_gw = nc.dram_tensor("gw", [I, D], F32, kind="ExternalInput").ap()
    ap_uw = nc.dram_tensor("uw", [I, D], F32, kind="ExternalInput").ap()
    ap_dw = nc.dram_tensor("dw", [D, I], F32, kind="ExternalInput").ap()
    ap_sin = nc.dram_tensor("s_in", [1, 1], F32, kind="ExternalInput").ap()
    ap_sind = nc.dram_tensor("s_in_down", [1, 1], F32, kind="ExternalInput").ap()
    ap_tri = nc.dram_tensor("tri", [128, 128], BF16, kind="ExternalInput").ap()
    ap_triT = nc.dram_tensor("triT", [128, 128], BF16, kind="ExternalInput").ap()
    ap_negI = nc.dram_tensor("negI", [128, 128], BF16, kind="ExternalInput").ap()
    ap_e2 = nc.dram_tensor("e2", [128, 2], BF16, kind="ExternalInput").ap()
    ap_ones = nc.dram_tensor("onesc", [128, 128], BF16, kind="ExternalInput").ap()
    ap_y = nc.dram_tensor("y", [SPC, S, D], F32, kind="ExternalOutput").ap()

    import os as _os0
    _IKIND = "ExternalOutput" if _os0.environ.get("KDBG2") == "1" else "Internal"
    ap_gwq = nc.dram_tensor("gw_q", [I, D], BF16, kind=_IKIND).ap()
    ap_uwq = nc.dram_tensor("uw_q", [I, D], BF16, kind=_IKIND).ap()
    ap_dwq = nc.dram_tensor("dw_q", [D, I], BF16, kind=_IKIND).ap()
    ap_sscr = nc.dram_tensor("s_scr", [TPC, D], F16, kind=_IKIND).ap()
    ap_brel = nc.dram_tensor("b_rel", [SPC, 16], F32, kind="Internal").ap()
    ap_rrel = nc.dram_tensor("rl_rel", [2, 512], F32, kind="Internal").ap()
    ap_grel = nc.dram_tensor("g_rel", [4, 128], F32, kind="Internal").ap()
    ap_srel = nc.dram_tensor("s_rel", [1, 8], F32, kind="Internal").ap()
    ap_dxnf = nc.dram_tensor("d_xnf", [128, 8 * S], BF16, kind=_IKIND).ap()
    ap_dQ = nc.dram_tensor("d_Q", [128, 8 * S], BF16, kind=_IKIND).ap()
    ap_dnb = nc.dram_tensor("d_nb", [128, 16], F32, kind=_IKIND).ap()
    ap_dAV = nc.dram_tensor("d_AV", [128, 8 * S], BF16, kind=_IKIND).ap()
    import os as _os
    DBG = _os.environ.get("KDBG") == "1"
    if DBG:
        ap_dxq = nc.dram_tensor("d_xq", [ST, D], BF16, kind="Internal").ap()
        ap_dhid = nc.dram_tensor("d_hid", [ST, I], F16, kind="Internal").ap()
        ap_dhq = nc.dram_tensor("d_hq", [ST, I], BF16, kind="Internal").ap()
        ap_dg = nc.dram_tensor("d_g", [ST, D], F16, kind="Internal").ap()

    K = KB(nc)
    for sname in ("hD", "gD", "tD", "scD", "T", "V", "Sc"):
        K.new_sem(es, sname)

    def sb(stack, name, shape, dt):
        return stack.enter_context(nc.sbuf_tensor(name, shape, dt))

    t_triT = sb(es, "triT_sb", [128, 128], BF16)
    t_negI = sb(es, "negI_sb", [128, 128], BF16)
    t_e2 = sb(es, "e2_sb", [128, 2], BF16)
    t_ones = sb(es, "ones_sb", [128, 64], BF16)
    t_scal = sb(es, "scal_sb", [128, 8], F32)

    psum = [es.enter_context(nc.psum_tensor(f"pb{i}", [128, 512], F32))
            for i in range(8)]

    with nc.allow_low_precision(reason="fp16 quantizer / bf16 matmul pipeline"):
        blk = es.enter_context(nc.Block())

        for (dst, src) in ((t_e2[:], ap_e2), (t_ones[:], ap_ones[:, 0:64]),
                           (t_triT[:], ap_triT), (t_negI[:], ap_negI),
                           (t_scal[0:1, 0:1], ap_sin), (t_scal[0:1, 1:2], ap_sind)):
            K.emit("sync", lambda e, dst=dst, src=src: e.dma_start(out=dst, in_=src),
                   inc="hD", inc_by=16)
        consts_ld = K.cnt["hD"]

        K.emit("vector", lambda e: e.tensor_scalar(t_scal[0:1, 6:7], t_scal[0:1, 0:1],
                                                   0.0, 6.0, op.mult, op.add),
               wait=[("hD", consts_ld)])
        K.emit("vector", lambda e: e.reciprocal(t_scal[0:1, 7:8], t_scal[0:1, 6:7]))
        K.emit("vector", lambda e: e.tensor_single_scalar(t_scal[0:1, 2:3],
                                                          t_scal[0:1, 0:1],
                                                          t_scal[0:1, 7:8], op.mult))
        K.emit("vector", lambda e: e.tensor_single_scalar(t_scal[0:1, 3:4],
                                                          t_scal[0:1, 1:2],
                                                          t_scal[0:1, 7:8], op.mult))
        K.emit("vector", lambda e: e.reciprocal(t_scal[0:1, 4:5], t_scal[0:1, 0:1]))
        sprep = K.emit("vector", lambda e: e.reciprocal(t_scal[0:1, 5:6],
                                                        t_scal[0:1, 1:2]), inc="V")
        w1 = K.emit("sync", lambda e: e.dma_start(out=ap_srel, in_=t_scal[0:1, 0:8]),
                    wait=[("V", sprep)], inc="hD", inc_by=16)
        K.emit("sync", lambda e: e.dma_start(out=t_scal[:, 0:8],
                                             in_=ap_srel.broadcast_to([128, 8])),
               wait=[("hD", w1)], inc="hD", inc_by=16)
        scal_rdy = K.emit("vector",
                          lambda e: e.memset(t_scal[0:1, 6:7], 0.0),
                          wait=[("hD", K.cnt["hD"])], inc="V")
        GS6_A, GS6_D = t_scal[:, 2:3], t_scal[:, 3:4]
        GSI_A, GSI_D = t_scal[:, 4:5], t_scal[:, 5:6]

        wq_stack = contextlib.ExitStack()
        w_ld = [sb(wq_stack, f"wld{i}", [128, 1024], F32) for i in range(3)]
        w_amA = sb(wq_stack, "w_amA", [128, 2048], F32)
        w_dq = sb(wq_stack, "w_dq", [128, 512], F32)
        w_rdq = sb(wq_stack, "w_rdq", [128, 512], F32)
        w_gm = sb(wq_stack, "w_gm", [128, 4], F32)
        w_gs = sb(wq_stack, "w_gs", [128, 8], F32)
        w_rel = w_dq
        w_pool = {c: sb(wq_stack, f"wsc{c}", [128, 512], F32)
                  for c in ("S0", "T0", "T1", "V0", "G", "GI")}
        w_pool["T2"] = w_pool["G"]
        w_sc = w_pool
        w_qt = {k: w_pool[c] for k, c in
                zip("ABCDEF", ("S0", "T0", "T1", "T2", "V0", "S0"))}

        def gen_weight_units():
            wdefs = [(ap_gw, ap_gwq, [(t * 128, 0) for t in range(32)], 0),
                     (ap_uw, ap_uwq, [(t * 128, 0) for t in range(32)], 1),
                     (ap_dw, ap_dwq, [(t * 128, j * 1024) for t in range(8)
                                      for j in range(4)], 2)]
            NW = 3
            cast_done = [0] * NW
            store_done = [0] * NW
            for (aw, awq, tiles, widx) in wdefs:
                amv = 0
                for iu, (r0, c0) in enumerate(tiles):
                    buf = iu % NW
                    ld = K.emit("sync", lambda e, r0=r0, c0=c0, buf=buf, aw=aw:
                                e.dma_start(out=w_ld[buf][:],
                                            in_=aw[r0:r0 + 128, c0:c0 + 1024]),
                                wait=[("V", cast_done[buf]),
                                      ("hD", store_done[buf])],
                                inc="hD", inc_by=16)
                    amv = K.emit("vector", lambda e, iu=iu, buf=buf:
                                 e.tensor_reduce(
                                     w_amA[:, iu * 64:(iu + 1) * 64],
                                     w_ld[buf][:].rearrange("p (b k) -> p b k", k=16),
                                     AX, op.max, apply_absolute_value=True),
                                 wait=[("hD", ld)], inc="V")
                    cast_done[buf] = amv
                    if iu % 4 == 3:
                        yield None
                gmv = K.emit("vector", lambda e:
                             e.tensor_reduce(w_gm[:, 0:1], w_amA[:], AX, op.max),
                             wait=[("V", amv)], inc="V")
                wg1 = K.emit("sync", lambda e, widx=widx:
                             e.dma_start(out=ap_grel[widx, :].unsqueeze(0),
                                         in_=w_gm[:, 0:1]),
                             wait=[("V", gmv)], inc="hD", inc_by=16)
                ld2 = K.emit("sync", lambda e, widx=widx:
                             e.dma_start(out=w_rel[0:1, 0:128],
                                         in_=ap_grel[widx, :].unsqueeze(0)),
                             wait=[("hD", wg1)], inc="hD", inc_by=16)

                K.emit("vector", lambda e:
                       e.tensor_reduce(w_gm[0:1, 1:2], w_rel[0:1, 0:128], AX, op.max),
                       wait=[("hD", ld2)])
                K.emit("vector", lambda e: e.reciprocal(w_gm[0:1, 2:3], w_gm[0:1, 1:2]))
                K.emit("vector", lambda e: e.tensor_single_scalar(w_gm[0:1, 2:3],
                                                                  w_gm[0:1, 2:3],
                                                                  2688.0, op.mult))
                K.emit("vector", lambda e: e.tensor_single_scalar(w_gm[0:1, 3:4],
                                                                  w_gm[0:1, 2:3],
                                                                  t_scal[0:1, 7:8], op.mult))
                gsv = K.emit("vector", lambda e: e.reciprocal(w_gm[0:1, 0:1],
                                                              w_gm[0:1, 2:3]), inc="V")
                wg2 = K.emit("sync", lambda e: e.dma_start(out=ap_grel[3, 0:4].unsqueeze(0),
                                                           in_=w_gm[0:1, 0:4]),
                             wait=[("V", gsv)], inc="hD", inc_by=16)
                K.emit("sync", lambda e, widx=widx:
                       e.dma_start(out=w_gs[:, 2 * widx:2 * widx + 1],
                                   in_=ap_grel[3, 3:4].unsqueeze(0).broadcast_to([128, 1])),
                       wait=[("hD", wg2)], inc="hD", inc_by=16)
                gsb_h = K.emit("sync", lambda e, widx=widx:
                               e.dma_start(out=w_gs[:, 2 * widx + 1:2 * widx + 2],
                                           in_=ap_grel[3, 0:1].unsqueeze(0).broadcast_to([128, 1])),
                               inc="hD", inc_by=16)
                gsb = K.emit("vector", lambda e:
                             e.memset(w_gm[:, 2:3], 0.0),
                             wait=[("hD", gsb_h)], inc="V")
                yield None
                pend_st = []

                def flush_store():
                    buf2, r2, c2, qd2 = pend_st.pop(0)
                    store_done[buf2] = K.emit(
                        "sync", lambda e, r2=r2, c2=c2, buf2=buf2, awq=awq:
                        e.dma_start(out=awq[r2:r2 + 128, c2:c2 + 1024],
                                    in_=w_ld[buf2].bitcast(BF16)[:, 0:1024]),
                        wait=[("V", qd2)], inc="hD", inc_by=16)

                for g0 in range(0, len(tiles), 8):
                    grp = tiles[g0:g0 + 8]
                    scv = emit_scale_pipeline(
                        K, w_amA[:, g0 * 64:(g0 + len(grp)) * 64],
                        w_gs[:, 2 * widx:2 * widx + 1],
                        w_gs[:, 2 * widx + 1:2 * widx + 2],
                        w_dq[:, 0:len(grp) * 64], w_rdq[:, 0:len(grp) * 64],
                        {k: t[:, 0:len(grp) * 64] for k, t in w_sc.items()},
                        wait=[("V", gsb)])
                    yield None
                    for j, (r0, c0) in enumerate(grp):
                        buf = (g0 + j) % NW
                        if len(pend_st) >= NW - 1:
                            flush_store()
                        ld = K.emit("sync",
                                    lambda e, r0=r0, c0=c0, buf=buf, aw=aw:
                                    e.dma_start(out=w_ld[buf][:],
                                                in_=aw[r0:r0 + 128, c0:c0 + 1024]),
                                    wait=[("V", cast_done[buf]),
                                          ("hD", store_done[buf])],
                                    inc="hD", inc_by=16)
                        qd = 0
                        for hslice in range(2):
                            qd = emit_quant_apply(
                                K,
                                w_ld[buf][:, hslice * 512:(hslice + 1) * 512],
                                w_rdq[:, j * 64 + hslice * 32: j * 64 + (hslice + 1) * 32],
                                w_dq[:, j * 64 + hslice * 32: j * 64 + (hslice + 1) * 32],
                                w_ld[buf].bitcast(BF16)[:, hslice * 512:(hslice + 1) * 512],
                                {k: t[:] for k, t in w_qt.items()},
                                wait=([("V", scv), ("hD", ld)]
                                      if hslice == 0 else []))
                        cast_done[buf] = qd
                        pend_st.append((buf, r0, c0, qd))
                        yield None
                while pend_st:
                    flush_store()
            while True:
                yield None

        wq_gen = gen_weight_units()

        def wq_step(n=1):
            for _ in range(n):
                next(wq_gen)

        att = contextlib.ExitStack()
        a_xld = [sb(att, f"a_xld{i}", [128, D], F32) for i in range(2)]
        a_stat = sb(att, "a_stat", [128, 8], F32)
        a_xn = [sb(att, f"a_xn{i}", [128, D], BF16) for i in range(2)]
        a_wld = [sb(att, f"a_wld{i}", [128, 1024], BF16) for i in range(2)]
        a_sqw = sb(att, "a_sqw", [128, 1024], BF16)
        a_Q = sb(att, "a_Q", [128, 8 * S], BF16)
        a_Kt = sb(att, "a_K", [128, 8 * S], BF16)
        a_vw = sb(att, "a_vw", [128, 8 * 1024], BF16)
        a_V = sb(att, "a_V", [128, 16 * (H * 65)], BF16)
        a_PT = [sb(att, f"a_PT{i}", [128, 512], BF16) for i in range(4)]
        a_nrm = sb(att, "a_nrm", [128, 20], F32)
        a_nbb = sb(att, "a_nbb", [128, 16], F32)
        a_rlb = sb(att, "a_rlb", [128, 512], F32)
        a_stg = sb(att, "a_stg", [128, 512], BF16)
        a_so = [sb(att, "a_so0", [128, 512], F16)] * 2
        a_xnfm = sb(att, "a_xnfm", [128, 8 * S], BF16)
        a_AV = a_xnfm
        rlb_ap = {0: a_rlb[0:64, 0:512],
                  1: a_sqw.bitcast(F32)[0:64, 0:512]}

        seq_end = {s: 0 for s in K.cnt}
        for seq in range(SPC):
            seq_wait = list(seq_end.items()) if seq > 0 else []
            ln_apply = []
            tp_done = []
            for sc in range(16):
                buf = sc % 2
                pend = list(seq_wait) if sc < 2 else [("V", ln_apply[sc - 2])]
                ld = K.emit("sync", lambda e, seq=seq, sc=sc, buf=buf:
                            e.dma_start(out=a_xld[buf][:],
                                        in_=ap_x[seq, sc * 128:(sc + 1) * 128, :]),
                            wait=pend, inc="hD", inc_by=16)
                lw = [("hD", ld)]
                if sc >= 2:
                    lw.append(("tD", tp_done[sc - 2]))
                elif seq_wait:
                    lw += seq_wait
                xnv = emit_ln_chunk(K, a_xld[buf][:], a_xn[buf][:], a_stat,
                                    a_sqw[:, 0:1024], wait_load=lw)
                ln_apply.append(xnv)

                def _tp(e, sc=sc, buf=buf):
                    o3 = a_xnfm[:].rearrange("p (c s) -> p c s", s=S)[:, :, sc * 128:(sc + 1) * 128]
                    i3 = a_xn[buf][:].rearrange("p (c f) -> p c f", f=128)
                    return e.dma_start(out=o3, in_=i3, transpose=True)

                tpv = K.emit("scalar", _tp, wait=[("V", xnv)], inc="tD", inc_by=16)
                tp_done.append(tpv)
                if sc % 4 == 3:
                    wq_step()
            xn_rdy = tp_done[-1]
            if seq == 0 and _IKIND == "ExternalOutput":
                K.emit("sync", lambda e: e.dma_start(out=ap_dxnf, in_=a_xnfm[:]),
                       wait=[("tD", xn_rdy)], inc="hD", inc_by=16)

            qk_evac = []
            qk_mm_last = [0] * 16
            li = 0
            for (wap, dst, scale) in ((ap_qT, a_Q, 0.125), (ap_kT, a_Kt, 1.0)):
                for hkc in range(8):
                    buf = li % 2
                    pend = ([("T", qk_mm_last[li - 2])] if li >= 2 else list(seq_wait))
                    ld = K.emit("gpsimd", lambda e, wap=wap, hkc=hkc, buf=buf:
                                e.dma_start(out=a_wld[buf][:], in_=wap[hkc]),
                                wait=pend, inc="gD", inc_by=16)
                    mmv = None
                    for nq in range(NQ):
                        bank = (hkc * NQ + nq) % 2
                        pw = [("gD", ld), ("tD", xn_rdy)]
                        if len(qk_evac) >= 2:
                            pw.append(("Sc", qk_evac[-2]))
                        for dc in range(8):
                            mmv = K.emit(
                                "tensor",
                                lambda e, bank=bank, dc=dc, nq=nq, buf=buf:
                                e.matmul(psum[bank][:],
                                         a_wld[buf][:].rearrange("p (c n) -> p c n", n=128)[:, dc, :],
                                         a_xnfm[:, dc * S + nq * 512: dc * S + (nq + 1) * 512],
                                         start=(dc == 0), stop=(dc == 7)),
                                wait=(pw if dc == 0 else ()),
                                inc=("T" if dc == 7 else None))
                        ev = K.emit("scalar",
                                    lambda e, bank=bank, dst=dst, hkc=hkc, nq=nq, scale=scale:
                                    e.activation(dst[:, hkc * S + nq * 512: hkc * S + (nq + 1) * 512],
                                                 psum[bank][:], AF.Copy, scale=scale),
                                    wait=[("T", mmv)], inc="Sc")
                        qk_evac.append(ev)
                    qk_mm_last[li] = mmv
                    li += 1
                    if hkc % 4 == 3:
                        wq_step()
            qk_rdy = qk_evac[-1]
            if seq == 0 and _IKIND == "ExternalOutput":
                K.emit("sync", lambda e: e.dma_start(out=ap_dQ, in_=a_Q[:]),
                       wait=[("Sc", qk_rdy)], inc="hD", inc_by=16)

            vld = []
            for dc in range(8):
                ld = K.emit("gpsimd", lambda e, dc=dc:
                            e.dma_start(out=a_vw[:, dc * 1024:(dc + 1) * 1024],
                                        in_=ap_vT[dc * 128:(dc + 1) * 128, :]),
                            wait=(list(seq_wait) if dc == 0 else ()),
                            inc="gD", inc_by=16)
                vld.append(ld)
            vones = 0
            for h in range(H):
                col = h * 65 + 64
                vones = K.emit("vector", lambda e, col=col:
                               e.memset(a_V[:].rearrange("p (c n) -> p c n", n=H * 65)[:, :, col:col + 1], 1.0),
                               wait=(list(seq_wait) if h == 0 else ()), inc="V")
            vt_evac = []
            vt_mm_last = 0
            for sc in range(16):
                for nv in range(2):
                    bank = 2 + (sc * 2 + nv) % 2
                    pw = [("tD", xn_rdy), ("V", vones), ("gD", vld[-1])]
                    if len(vt_evac) >= 2:
                        pw.append(("Sc", vt_evac[-2]))
                    mmv = None
                    for dc in range(8):
                        mmv = K.emit(
                            "tensor",
                            lambda e, bank=bank, sc=sc, nv=nv, dc=dc:
                            e.matmul(psum[bank][:],
                                     a_xnfm[:, dc * S + sc * 128: dc * S + sc * 128 + 128],
                                     a_vw[:, dc * 1024 + nv * 512: dc * 1024 + (nv + 1) * 512],
                                     start=(dc == 0), stop=(dc == 7)),
                            wait=(pw if dc == 0 else ()),
                            inc=("T" if dc == 7 else None))
                    vt_mm_last = mmv

                    ev = None
                    for j in range(8):
                        off = sc * (H * 65) + (nv * 8 + j) * 65
                        ev = K.emit("scalar",
                                    lambda e, bank=bank, off=off, j=j:
                                    e.copy(a_V[:, off:off + 64],
                                           psum[bank][:, j * 64:(j + 1) * 64]),
                                    wait=([("T", mmv)] if j == 0 else ()),
                                    inc=("Sc" if j == 7 else None))
                    vt_evac.append(ev)
                if sc % 4 == 3:
                    wq_step()
            v_rdy = vt_evac[-1]

            nrm_last = 0
            for (qsrc, cb) in ((a_Q, 0), (a_Kt, 8)):
                for hc in range(8):
                    qmax = []
                    for hhf in range(2):
                        sqv = K.emit("scalar", lambda e, qsrc=qsrc, hc=hc, hhf=hhf:
                                     e.activation(a_sqw[:],
                                                  qsrc[:, hc * S + hhf * 1024:
                                                       hc * S + (hhf + 1) * 1024],
                                                  AF.Square),
                                     wait=[("Sc", v_rdy)] +
                                          ([("V", qmax[-1])] if qmax else
                                           ([("V", nrm_last)] if nrm_last else [])),
                                     inc="Sc")
                        for qq in range(2):
                            mmv = K.emit("tensor", lambda e, qq=qq:
                                         e.matmul(psum[6][0:2, 0:512], t_e2[:],
                                                  a_sqw[:, qq * 512:(qq + 1) * 512],
                                                  start=True, stop=True),
                                         wait=[("Sc", sqv)] + ([("V", qmax[-1])] if qmax else []),
                                         inc="T")
                            rv = K.emit("vector", lambda e, hhf=hhf, qq=qq:
                                        e.tensor_reduce(a_nrm[0:2, 16 + hhf * 2 + qq:
                                                              17 + hhf * 2 + qq],
                                                        psum[6][0:2, 0:512], AX, op.max),
                                        wait=[("T", mmv)], inc="V")
                            qmax.append(rv)
                    nrm_last = K.emit("vector", lambda e, cb=cb, hc=hc:
                                      e.tensor_reduce(a_nrm[0:2, cb + hc:cb + hc + 1],
                                                      a_nrm[0:2, 16:20], AX, op.max),
                                      inc="V")

            K.emit("vector", lambda e:
                   e.tensor_tensor(a_nrm[0:2, 0:8], a_nrm[0:2, 0:8],
                                   a_nrm[0:2, 8:16], op.mult),
                   wait=[("V", nrm_last)])
            bv = K.emit("vector", lambda e:
                        e.tensor_single_scalar(a_nrm[0:2, 0:8], a_nrm[0:2, 0:8],
                                               1e-12, op.max), inc="V")
            bsq = K.emit("scalar", lambda e: e.activation(a_nrm[0:2, 8:16],
                                                          a_nrm[0:2, 0:8], AF.Sqrt),
                         wait=[("V", bv)], inc="Sc")
            bneg = K.emit("vector", lambda e:
                          e.tensor_scalar(a_nrm[0:2, 8:16], a_nrm[0:2, 8:16],
                                          0.1, -1.0, op.add, op.mult),
                          wait=[("Sc", bsq)], inc="V")
            wb1 = K.emit("sync", lambda e, seq=seq:
                         e.dma_start(out=ap_brel[seq, :].unsqueeze(0),
                                     in_=a_nrm[0:2, 8:16]),
                         wait=[("V", bneg)], inc="hD", inc_by=16)
            bbh = K.emit("sync", lambda e, seq=seq:
                         e.dma_start(out=a_nbb[:],
                                     in_=ap_brel[seq, :].unsqueeze(0).broadcast_to([128, 16])),
                         wait=[("hD", wb1)], inc="hD", inc_by=16)
            nbb_rdy = K.emit("scalar", lambda e: e.memzero(a_nrm[:, 16:17]),
                             wait=[("hD", bbh)], inc="Sc")

            if seq == 0 and _IKIND == "ExternalOutput":
                K.emit("sync", lambda e: e.dma_start(out=ap_dnb, in_=a_nbb[:]),
                       wait=[("Sc", nbb_rdy)], inc="hD", inc_by=16)
            exp_bank = {}
            av_mm = {}
            bank_free = {}
            abank_of = {0: (4, 6), 1: (5, 7)}
            relay_st = {0: 0, 1: 0}
            ev_of = {0: 0, 1: 0}
            ev_last = 0
            shift_last = 0

            def emit_avepi1(hc, qq, avv):
                rds = {}
                for par in range(2):
                    abank = abank_of[par][qq % 2]
                    rw = [("T", avv[par])]
                    if relay_st[1 - par]:
                        rw.append(("gD", relay_st[1 - par]))
                    rcv = K.emit("vector", lambda e, abank=abank:
                                 e.reciprocal(a_rlb[64:65, 0:512],
                                              psum[abank][64:65, 0:512]),
                                 wait=rw, inc="V")
                    st = K.emit("gpsimd", lambda e, seq=seq, par=par:
                                e.dma_start(out=ap_rrel[par, :].unsqueeze(0),
                                            in_=a_rlb[64:65, 0:512]),
                                wait=[("V", rcv)], inc="gD", inc_by=16)
                    relay_st[par] = st
                    rw2 = [("gD", st)]
                    if ev_of[par]:
                        rw2.append(("V", ev_of[par]))
                    rds[par] = K.emit("gpsimd", lambda e, par=par:
                                      e.dma_start(out=rlb_ap[par],
                                                  in_=ap_rrel[par, :].unsqueeze(0)
                                                  .broadcast_to([64, 512])),
                                      wait=rw2, inc="gD", inc_by=16)
                return rds

            def emit_avepi2(hc, qq, rds):
                nonlocal ev_last, shift_last
                qq = qq
                evs = {}
                for par in range(2):
                    abank = abank_of[par][qq % 2]
                    if par == 0:
                        dst_ap = a_AV[0:64, hc * S + qq * 512: hc * S + (qq + 1) * 512]
                        evw = [("gD", rds[par])]
                    else:
                        dst_ap = a_stg[0:64, 0:512]
                        evw = [("gD", rds[par])] + ([("gD", shift_last)] if shift_last else [])
                    ev = K.emit(
                        "vector",
                        lambda e, abank=abank, dst_ap=dst_ap, par=par, qq=qq:
                        e.scalar_tensor_tensor(dst_ap, psum[abank][0:64, 0:512],
                                               1.0, rlb_ap[par],
                                               op.mult, op.mult),
                        wait=evw, inc="V")
                    evs[par] = ev
                    bank_free[(par, qq % 2)] = ev
                    ev_of[par] = ev
                    ev_last = ev
                shift_last = K.emit(
                    "gpsimd",
                    lambda e, hc=hc, qq=qq:
                    e.dma_start(out=a_AV[64:128, hc * S + qq * 512:
                                         hc * S + (qq + 1) * 512],
                                in_=a_stg[0:64, 0:512]),
                    wait=[("V", evs[1])], inc="gD", inc_by=16)

            pend1 = None
            pend2 = None
            for hc in range(8):
                for qq in range(NQ):
                    if pend2 is not None:
                        emit_avepi2(*pend2)
                        pend2 = None
                    if pend1 is not None:
                        rds = emit_avepi1(*pend1)
                        pend2 = (pend1[0], pend1[1], rds)
                        pend1 = None
                    nkc = 4 * qq + 4
                    avv = {}
                    for kc in range(nkc):
                        kbuf = kc % 2
                        pbuf = kc % 2
                        if qq >= 1 and kc == 2 * qq + 2:
                            wq_step(1)
                        if qq >= 2 and kc == qq:
                            wq_step(1)
                        r = 128 * kc - 512 * qq
                        r0 = max(r, 0)
                        qn = 512 - r0
                        smm = {}
                        for par in range(2):
                            key = (par, kbuf)
                            bank = (2 if par == 0 else 0) + kbuf
                            smw = [("Sc", nbb_rdy)]
                            if key in exp_bank:
                                smw.append(("Sc", exp_bank[key]))
                            mm = K.emit(
                                "tensor",
                                lambda e, bank=bank, hc=hc, par=par, kc=kc, qq=qq, r0=r0, qn=qn, r=r:
                                e.matmul(psum[bank][:, r0:r0 + qn],
                                         a_Kt[par * 64:par * 64 + 64,
                                              hc * S + kc * 128: hc * S + (kc + 1) * 128],
                                         a_Q[par * 64:par * 64 + 64,
                                             hc * S + qq * 512 + r0: hc * S + qq * 512 + r0 + qn],
                                         start=True, stop=(r < 0),
                                         tile_position=(par * 64, 0)),
                                wait=smw, inc=(None if r >= 0 else "T"))
                            if r >= 0:
                                mm = K.emit(
                                    "tensor",
                                    lambda e, bank=bank, r=r:
                                    e.matmul(psum[bank][:, r:r + 128],
                                             t_triT[:], t_negI[:],
                                             start=False, stop=True),
                                    inc="T")
                            smm[par] = mm
                        exs = {}
                        for par in range(2):
                            key = (par, pbuf)
                            pt = a_PT[2 * par + pbuf]
                            bank = (2 if par == 0 else 0) + kbuf
                            h = 2 * hc + par
                            jcol = (h % 2) * 8 + h // 2
                            ew = [("T", smm[par])]
                            if key in av_mm:
                                ew.append(("T", av_mm[key]))
                            exs[par] = K.emit(
                                "scalar",
                                lambda e, pt=pt, bank=bank, r0=r0, jcol=jcol:
                                e.activation(pt[:, r0:512],
                                             psum[bank][:, r0:512],
                                             AF.Exp,
                                             bias=a_nbb[:, jcol:jcol + 1]),
                                wait=ew, inc="Sc")
                            exp_bank[(par, kbuf)] = exs[par]
                        for par in range(2):
                            key = (par, pbuf)
                            pt = a_PT[2 * par + pbuf]
                            h = 2 * hc + par
                            abank = abank_of[par][qq % 2]
                            avw = [("Sc", exs[par])]
                            if kc == 0:
                                bk = bank_free.get((par, qq % 2), 0)
                                if bk:
                                    avw.append(("V", bk))
                            mm = K.emit(
                                "tensor",
                                lambda e, abank=abank, kc=kc, h=h, pt=pt, nkc=nkc, r0=r0:
                                e.matmul(psum[abank][0:65, r0:512],
                                         a_V[:, kc * (H * 65) + h * 65:
                                             kc * (H * 65) + h * 65 + 65],
                                         pt[:, r0:512],
                                         start=(kc == 0), stop=(kc == nkc - 1)),
                                wait=avw, inc="T")
                            av_mm[key] = mm
                            avv[par] = mm
                    pend1 = (hc, qq, avv)
                    wq_step(1)
            if pend2 is not None:
                emit_avepi2(*pend2)
            rds = emit_avepi1(*pend1)
            emit_avepi2(pend1[0], pend1[1], rds)
            pend1 = None
            pend2 = None
            av_rdy_h = shift_last
            av_rdy_v = ev_last

            old = []
            for pc in range(8):
                ld = K.emit("gpsimd", lambda e, pc=pc:
                            e.dma_start(out=a_vw[:, pc * 1024:(pc + 1) * 1024],
                                        in_=ap_oT[pc * 128:(pc + 1) * 128, :]),
                            wait=([("T", vt_mm_last)] if pc == 0 else ()),
                            inc="gD", inc_by=16)
                old.append(ld)
            so_ev = []
            so_dma = []
            for qc in range(16):
                for half in range(2):
                    buf = (qc * 2 + half) % 2
                    bank = 2 + (qc * 2 + half) % 2
                    pw = [("gD", av_rdy_h), ("V", av_rdy_v), ("gD", old[-1])]
                    if len(so_ev) >= 2:
                        pw.append(("Sc", so_ev[-2]))
                    if len(so_dma) >= 2:
                        pw.append(("scD", so_dma[-2]))
                    mmv = None
                    for pc in range(8):
                        mmv = K.emit(
                            "tensor",
                            lambda e, bank=bank, qc=qc, half=half, pc=pc:
                            e.matmul(psum[bank][:],
                                     a_AV[:, pc * S + qc * 128: pc * S + qc * 128 + 128],
                                     a_vw[:, pc * 1024 + half * 512: pc * 1024 + half * 512 + 512],
                                     start=(pc == 0), stop=(pc == 7)),
                            wait=(pw if pc == 0 else ()),
                            inc=("T" if pc == 7 else None))
                    ev = K.emit("scalar", lambda e, bank=bank, buf=buf:
                                e.copy(a_so[buf][:], psum[bank][:]),
                                wait=[("T", mmv)] +
                                     ([("scD", so_dma[-1])] if so_dma else []),
                                inc="Sc")
                    so_ev.append(ev)
                    dmv = K.emit("scalar", lambda e, seq=seq, qc=qc, half=half, buf=buf:
                                 e.dma_start(out=ap_sscr[seq * S + qc * 128:
                                                         seq * S + (qc + 1) * 128,
                                                         half * 512:(half + 1) * 512],
                                             in_=a_so[buf][:]),
                                 wait=[("Sc", so_ev[-1])], inc="scD", inc_by=16)
                    so_dma.append(dmv)
                if qc % 4 == 3:
                    wq_step()
            seq_end = K.counts()

        for _ in range(256):
            wq_step()
        att_end = K.counts()
        att.close()
        wq_stack.close()

        ffn = contextlib.ExitStack()
        f_stat = sb(ffn, "f_stat", [128, 16], F32)
        f_xam = sb(ffn, "f_xam", [128, 256], F32)
        f_xdq = sb(ffn, "f_xdq", [128, 256], F32)
        f_xrdq = sb(ffn, "f_xrdq", [128, 256], F32)
        f_xqT = sb(ffn, "f_xqT", [128, 8 * 512], BF16)
        f_gwT = [sb(ffn, f"f_gwT{i}", [128, 8 * 512], BF16) for i in range(2)]
        f_uwT = [sb(ffn, f"f_uwT{i}", [128, 8 * 512], BF16) for i in range(2)]
        f_hid = sb(ffn, "f_hid", [128, 4 * I], F16)
        f_ham = sb(ffn, "f_ham", [128, 1024], F16)
        f_hdq = sb(ffn, "f_hdq", [128, 1024], F16)
        f_hrdq = sb(ffn, "f_hrdq", [128, 1024], F16)
        f_hq = sb(ffn, "f_hq", [128, I], BF16)
        f_hqT = sb(ffn, "f_hqT", [128, 32 * 512], BF16)
        f_dwT = sb(ffn, "f_dwT", [128, 16 * 512], BF16)
        f_x = [sb(ffn, f"f_x{i}", [128, 4 * D], F32) for i in range(2)]
        f_s = sb(ffn, "f_s", [128, 4 * 512], F16)
        f_out = [sb(ffn, f"f_out{i}", [128, 512], F32) for i in range(2)]
        f_g = sb(ffn, "f_g", [128, 512], F16)
        f_p32 = {c: sb(ffn, f"fp32{c}", [128, 1024], F32)
                 for c in ("S0", "T0", "T1", "T2", "V0", "G", "GI")}
        f_qt32 = {k: f_p32[c] for k, c in
                  zip("ABCDEF", ("S0", "T0", "T1", "T2", "V0", "S0"))}
        f_qt = {k: f_p32[c].bitcast(F16) for k, c in
                zip("ABCDEF", ("S0", "T0", "T1", "T2", "V0", "S0"))}
        f_t1 = f_p32["GI"]
        import os as _os2
        DBG = _os2.environ.get("KDBG") == "1"

        PS = [dict() for _ in range(NPASS + 1)]
        final_out = 0
        out_hist = []

        def emit_loads(p):
            seq = (p * ST) // S
            row0 = (p * ST) % S
            fx = f_x[p % 2]
            if p == 0:
                w0 = list(att_end.items())
            elif p >= 2:
                w0 = [("V", PS[p - 2]["cv_last"])]
            else:
                w0 = []
            xls = []
            for sc in range(4):
                ld = K.emit("sync", lambda e, seq=seq, row0=row0, sc=sc, fx=fx:
                            e.dma_start(out=fx[:, sc * D:(sc + 1) * D],
                                        in_=ap_x[seq, row0 + sc * 128: row0 + (sc + 1) * 128, :]),
                            wait=(w0 if sc == 0 else ()), inc="hD", inc_by=16)
                xls.append(ld)
            PS[p]["xls"] = xls

        def emit_sload(p, half, sw):
            PS[p]["sld%d" % half] = K.emit(
                "sync", lambda e, p=p, half=half:
                e.dma_start(out=f_s[:].rearrange("p (c n) -> p c n", n=512),
                            in_=ap_sscr[p * ST:(p + 1) * ST,
                                        half * 512:(half + 1) * 512].rearrange("(c p) n -> p c n", p=128)),
                wait=sw, inc="hD", inc_by=16)

        def gen_preamble(p):
            fx = f_x[p % 2]
            xls = PS[p]["xls"]
            ln_v = 0
            for sc in range(4):
                emit_ln_chunk(K, fx[:, sc * D:(sc + 1) * D],
                              f_t1[:], f_stat, f_xqT[:, 0:1024],
                              wait_load=[("hD", xls[sc])],
                              apply_out=False)
                K.emit("vector", lambda e, sc=sc:
                       e.tensor_copy(f_stat[:, 8 + 2 * sc:10 + 2 * sc],
                                     f_stat[:, 6:8]),
                       inc="V")
                K.emit("vector", lambda e, sc=sc, fx=fx:
                       e.tensor_scalar(f_t1[:], fx[:, sc * D:(sc + 1) * D],
                                       f_stat[:, 8 + 2 * sc:9 + 2 * sc],
                                       f_stat[:, 9 + 2 * sc:10 + 2 * sc],
                                       op.subtract, op.mult),
                       inc="V")
                ln_v = K.emit("vector", lambda e, sc=sc:
                              e.tensor_reduce(
                                  f_xam[:, sc * 64:(sc + 1) * 64],
                                  f_t1[:].rearrange("p (b k) -> p b k", k=16),
                                  AX, op.max, apply_absolute_value=True),
                              inc="V")
                yield
            xsc = emit_scale_pipeline(K, f_xam[:], GS6_A, GSI_A, f_xdq[:], f_xrdq[:],
                                      {k: t[:, 0:256] for k, t in f_p32.items()},
                                      wait=[("V", ln_v), ("V", scal_rdy)])
            yield
            xq_tp = []
            PS[p]["xq_tp"] = xq_tp
            samm = PS[p - 1].get("samm_last", 0) if p >= 1 else 0
            for sc in range(4):
                pw = ([("V", xsc)] if sc == 0 else [])
                pw += ([("tD", xq_tp[-1])] if xq_tp
                       else ([("tD", PS[p - 1]["hqT_rdy"])] if p >= 1 else []))
                K.emit("vector", lambda e, sc=sc, fx=fx:
                       e.tensor_scalar(f_t1[:], fx[:, sc * D:(sc + 1) * D],
                                       f_stat[:, 8 + 2 * sc:9 + 2 * sc],
                                       f_stat[:, 9 + 2 * sc:10 + 2 * sc],
                                       op.subtract, op.mult),
                       wait=pw, inc="V")
                qd = emit_quant_apply(
                    K, f_t1[:],
                    f_xrdq[:, sc * 64:(sc + 1) * 64],
                    f_xdq[:, sc * 64:(sc + 1) * 64],
                    f_hq[:, 0:1024],
                    {k: t[:, 0:1024] for k, t in f_qt32.items()})

                def _tpq(e, sc=sc):
                    o3 = f_xqT[:].rearrange("p (c s) -> p c s", s=512)[:, :, sc * 128:(sc + 1) * 128]
                    i3 = f_hq[:, 0:1024].rearrange("p (c f) -> p c f", f=128)
                    return e.dma_start(out=o3, in_=i3, transpose=True)

                tw = [("V", qd)] + ([("T", samm)] if (sc == 0 and samm) else [])
                tv = K.emit("sync", _tpq, wait=tw, inc="tD", inc_by=16)
                xq_tp.append(tv)
                if DBG and p == 0:
                    K.emit("sync", lambda e, sc=sc:
                           e.dma_start(out=ap_dxq[sc * 128:(sc + 1) * 128, :],
                                       in_=f_hq[:, 0:1024]),
                           wait=[("V", qd)], inc="hD", inc_by=16)
                yield
            while True:
                yield

        def emit_stageA(p, pre):
            xqT_rdy = PS[p]["xq_tp"][-1]
            ga_prev = PS[p - 1]["ga67"] if p >= 1 else (0, 0)
            ga_ev = list(PS[p - 1]["ga_tail"]) if p >= 1 else []
            gu_mm_last = [0] * 8
            samm = 0
            for ic in range(8):
                buf = ic % 2
                if ic >= 2:
                    pend = [("T", gu_mm_last[ic - 2])]
                else:
                    pend = [("tD", xqT_rdy)]
                    if ga_prev[buf]:
                        pend.append(("T", ga_prev[buf]))
                    if ic == 0 and out_hist:
                        pend.append(("hD", out_hist[-1]))

                def _ldg(e, ic=ic, buf=buf):
                    i3 = ap_gwq[ic * 512:(ic + 1) * 512, :].rearrange("i (c p) -> i c p", p=128)
                    o3 = f_gwT[buf][:].rearrange("p (c n) -> p c n", n=512)
                    return e.dma_start(out=o3, in_=i3, transpose=True)

                def _ldu(e, ic=ic, buf=buf):
                    i3 = ap_uwq[ic * 512:(ic + 1) * 512, :].rearrange("i (c p) -> i c p", p=128)
                    o3 = f_uwT[buf][:].rearrange("p (c n) -> p c n", n=512)
                    return e.dma_start(out=o3, in_=i3, transpose=True)

                K.emit("sync", _ldg, wait=pend, inc="tD", inc_by=16)
                ld = K.emit("sync", _ldu, inc="tD", inc_by=16)
                for ss in range(4):
                    gb = (ss % 2) * 2
                    ub = gb + 1
                    pw = [("tD", ld)]
                    if ic == 0 and ss == 0 and ga_ev:
                        pw.append(("V", ga_ev[-1]))
                    elif len(ga_ev) >= 2:
                        pw.append(("V", ga_ev[-2]))
                    mmv = None
                    for hcc in range(8):
                        K.emit("tensor",
                               lambda e, gb=gb, ss=ss, hcc=hcc, buf=buf:
                               e.matmul(psum[gb][:],
                                        f_xqT[:, hcc * 512 + ss * 128: hcc * 512 + ss * 128 + 128],
                                        f_gwT[buf][:, hcc * 512:(hcc + 1) * 512],
                                        start=(hcc == 0), stop=(hcc == 7)),
                               wait=(pw if hcc == 0 else ()), inc=None)
                        mmv = K.emit("tensor",
                                     lambda e, ub=ub, ss=ss, hcc=hcc, buf=buf:
                                     e.matmul(psum[ub][:],
                                              f_xqT[:, hcc * 512 + ss * 128: hcc * 512 + ss * 128 + 128],
                                              f_uwT[buf][:, hcc * 512:(hcc + 1) * 512],
                                              start=(hcc == 0), stop=(hcc == 7)),
                                     inc=("T" if hcc == 7 else None))
                    gu_mm_last[ic] = mmv
                    samm = mmv
                    sv = K.emit("scalar", lambda e, gb=gb, ss=ss, ic=ic:
                                e.activation(
                                    f_hid[:, ss * I + ic * 512: ss * I + (ic + 1) * 512],
                                    psum[gb][:], AF.Silu),
                                wait=[("T", mmv)], inc="Sc")
                    hv = K.emit("vector", lambda e, ss=ss, ic=ic, ub=ub:
                                e.tensor_tensor(
                                    f_hid[:, ss * I + ic * 512: ss * I + (ic + 1) * 512],
                                    f_hid[:, ss * I + ic * 512: ss * I + (ic + 1) * 512],
                                    psum[ub][:], op.mult),
                                wait=[("Sc", sv)], inc="V")
                    ga_ev.append(hv)
                    if ic == 7:
                        emit_hq_unit(p, ss)
            PS[p]["ga67"] = (gu_mm_last[6], gu_mm_last[7])
            PS[p]["ga_tail"] = ga_ev[-2:]
            PS[p]["hid_rdy"] = ga_ev[-1]
            PS[p]["samm_last"] = samm

        def emit_hq_unit(p, sc):
            K.emit("vector", lambda e, sc=sc:
                   e.tensor_reduce(
                       f_ham[:, sc * 256:(sc + 1) * 256],
                       f_hid[:, sc * I:(sc + 1) * I].rearrange("p (b k) -> p b k", k=16),
                       AX, op.max, apply_absolute_value=True),
                   inc="V")
            if sc % 2 == 0:
                return
            pr = sc // 2
            emit_scale_pipeline(K, f_ham[:, pr * 512:(pr + 1) * 512], GS6_D, GSI_D,
                                f_hdq[:, pr * 512:(pr + 1) * 512],
                                f_hrdq[:, pr * 512:(pr + 1) * 512],
                                {k: t[:, 0:512] for k, t in f_p32.items()},
                                wait=([("V", scal_rdy)] if (p == 0 and pr == 0) else ()))
            hq_half = PS[p].setdefault("tp", {})
            dguard = PS[p - 1]["dmm_last"] if p >= 1 else 0
            tv = 0
            for qc in (sc - 1, sc):
                for j in range(2):
                    hw = []
                    if qc == 0 and j == 0:
                        hw.append(("tD", PS[p]["xq_tp"][-1]))
                    if (qc - 1, j) in hq_half:
                        hw.append(("tD", hq_half[(qc - 1, j)]))
                    qd = emit_quant_apply(
                        K, f_hid[:, qc * I + j * 2048: qc * I + (j + 1) * 2048],
                        f_hrdq[:, qc * 256 + j * 128: qc * 256 + (j + 1) * 128],
                        f_hdq[:, qc * 256 + j * 128: qc * 256 + (j + 1) * 128],
                        f_hq[:, j * 2048:(j + 1) * 2048],
                        {k: t[:] for k, t in f_qt.items()},
                        wait=hw)

                    def _tph(e, qc=qc, j=j):
                        o3 = f_hqT[:].rearrange("p (c s) -> p c s", s=512)[
                            :, 16 * j:16 * (j + 1), qc * 128:(qc + 1) * 128]
                        i3 = f_hq[:, j * 2048:(j + 1) * 2048].rearrange(
                            "p (c f) -> p c f", f=128)
                        return e.dma_start(out=o3, in_=i3, transpose=True)

                    tw = [("V", qd)] + ([("T", dguard)] if (qc == 0 and dguard) else [])
                    tv = K.emit("sync", _tph, wait=tw, inc="tD", inc_by=16)
                    hq_half[(qc, j)] = tv
            PS[p]["hqT_rdy"] = tv

        def emit_stageB(p, pre):
            seq = (p * ST) // S
            row0 = (p * ST) % S
            fx = f_x[p % 2]
            dguard = PS[p - 1]["dmm_last"] if p >= 1 else 0
            dmm_last = 0
            ft_last = 0
            cv_last = PS[p - 1]["cv_last"] if p >= 1 else 0
            ob = 0
            npre = {(0, 0): 2, (0, 1): 2, (1, 0): 3, (1, 1): 2}
            for half in range(2):
                if half == 1:
                    emit_sload(p, 1, [("V", ft_last)])
                sld = PS[p]["sld%d" % half]
                for iq in range(2):
                    def _ldd(e, half=half, iq=iq):
                        i3 = ap_dwq[half * 512:(half + 1) * 512,
                                    iq * 2048:(iq + 1) * 2048].rearrange("h (c p) -> h c p", p=128)
                        o3 = f_dwT[:].rearrange("p (c n) -> p c n", n=512)
                        return e.dma_start(out=o3, in_=i3, transpose=True)

                    if half + iq:
                        dw = [("T", dmm_last)]
                    else:
                        dw = [("T", dguard)] if dguard else []
                    dld = K.emit("sync", _ldd, wait=dw, inc="tD", inc_by=16)
                    for ss in range(4):
                        bank = 4 + ss
                        pw = [("tD", dld), ("tD", PS[p]["tp"][(ss, iq)]),
                              ("hD", sld)]
                        mmv = None
                        for ic in range(16):
                            icg = iq * 16 + ic
                            mmv = K.emit(
                                "tensor",
                                lambda e, bank=bank, ss=ss, icg=icg, ic=ic, iq=iq:
                                e.matmul(psum[bank][:],
                                         f_hqT[:, icg * 512 + ss * 128: icg * 512 + ss * 128 + 128],
                                         f_dwT[:, ic * 512:(ic + 1) * 512],
                                         start=(iq == 0 and ic == 0),
                                         stop=(iq == 1 and ic == 15)),
                                wait=(pw if ic == 0 else ()),
                                inc=("T" if ic == 15 else None))
                        dmm_last = mmv
                        if iq == 0:
                            continue
                        gv = K.emit("scalar", lambda e, bank=bank:
                                    e.activation(f_g[:], psum[bank][:], AF.Silu),
                                    wait=[("T", mmv)] +
                                         ([("V", cv_last)] if cv_last else []),
                                    inc="Sc")
                        if DBG and p == 0:
                            K.emit("sync", lambda e, ss=ss, half=half:
                                   e.dma_start(out=ap_dg[ss * 128:(ss + 1) * 128,
                                                         half * 512:(half + 1) * 512],
                                               in_=f_g[:]),
                                   wait=[("Sc", gv)], inc="hD", inc_by=16)
                        tvw = [("Sc", gv)]
                        if len(out_hist) >= 2:
                            tvw.append(("hD", out_hist[-2]))
                        tv = K.emit("vector", lambda e, ss=ss:
                                    e.tensor_tensor(
                                        f_g[:], f_g[:],
                                        f_s[:, ss * 512:(ss + 1) * 512],
                                        op.mult),
                                    wait=tvw, inc="V")
                        ft_last = tv
                        obuf = ob % 2
                        cv = K.emit("vector", lambda e, ss=ss, half=half, obuf=obuf, fx=fx:
                                    e.scalar_tensor_tensor(
                                        f_out[obuf][:], f_g[:], 1.0,
                                        fx[:, ss * D + half * 512: ss * D + half * 512 + 512],
                                        op.mult, op.add),
                                    inc="V")
                        cv_last = cv
                        od = K.emit("scalar",
                                    lambda e, seq=seq, row0=row0, ss=ss, half=half, obuf=obuf:
                                    e.dma_start(
                                        out=ap_y[seq, row0 + ss * 128: row0 + (ss + 1) * 128,
                                                 half * 512:(half + 1) * 512],
                                        in_=f_out[obuf][:]),
                                    wait=[("V", cv)], inc="hD", inc_by=16)
                        out_hist.append(od)
                        ob += 1
                    if pre is not None:
                        for _ in range(npre[(half, iq)]):
                            next(pre)
            PS[p]["dmm_last"] = dmm_last
            PS[p]["ft_last"] = ft_last
            PS[p]["cv_last"] = cv_last
            return out_hist[-1]

        emit_loads(0)
        emit_sload(0, 0, [("scD", att_end["scD"])])
        pre = gen_preamble(0)
        for _ in range(9):
            next(pre)
        for p in range(NPASS):
            if p + 1 < NPASS:
                emit_loads(p + 1)
            nxt = gen_preamble(p + 1) if p + 1 < NPASS else None
            emit_stageA(p, nxt)
            final_out = emit_stageB(p, nxt)
            if p + 1 < NPASS:
                emit_sload(p + 1, 0, [("V", PS[p]["ft_last"])])

        hT = K.cnt["hD"]
        gT = K.cnt["gD"]
        K.emit("sync", lambda e: None,
               wait=[("hD", hT), ("gD", gT),
                     ("scD", K.cnt["scD"]), ("tD", K.cnt["tD"])])

        K.flush(blk)
        ffn.close()
    es.close()
    return nc


_BF = ml_dtypes.bfloat16


def kernel(L0_attn_X, L0_attn_q_proj, L0_attn_k_proj, L0_attn_v_proj,
           L0_attn_o_proj, L0_ffn_gate_w, L0_ffn_up_w, L0_ffn_down_w,
           L0_ffn_s_in, L0_ffn_s_in_down):
    x = np.ascontiguousarray(np.asarray(L0_attn_X, dtype=np.float32))
    _qt = np.asarray(L0_attn_q_proj, dtype=np.float32).T
    _kt = np.asarray(L0_attn_k_proj, dtype=np.float32).T
    qT = np.ascontiguousarray(_qt.reshape(8, 128, 8, 128).transpose(2, 1, 0, 3)
                              .reshape(8, 128, 1024))
    kT = np.ascontiguousarray(_kt.reshape(8, 128, 8, 128).transpose(2, 1, 0, 3)
                              .reshape(8, 128, 1024))
    vT = np.ascontiguousarray(np.asarray(L0_attn_v_proj, dtype=np.float32).T)
    oT = np.ascontiguousarray(np.asarray(L0_attn_o_proj, dtype=np.float32).T)
    gw = np.ascontiguousarray(np.asarray(L0_ffn_gate_w, dtype=np.float32))
    uw = np.ascontiguousarray(np.asarray(L0_ffn_up_w, dtype=np.float32))
    dw = np.ascontiguousarray(np.asarray(L0_ffn_down_w, dtype=np.float32))
    sin = np.asarray(L0_ffn_s_in, dtype=np.float32).reshape(1, 1)
    sind = np.asarray(L0_ffn_s_in_down, dtype=np.float32).reshape(1, 1)
    tri = np.triu(np.ones((128, 128), dtype=np.float32)).astype(_BF)
    e2 = np.zeros((128, 2), dtype=np.float32)
    e2[0:64, 0] = 1.0
    e2[64:128, 1] = 1.0
    e2 = e2.astype(_BF)
    ones = np.ones((128, 128), dtype=np.float32).astype(_BF)
    triT = np.triu(np.ones((128, 128), dtype=np.float32), 1).astype(_BF)
    negI = (-30000.0 * np.eye(128, dtype=np.float32)).astype(_BF)

    nc = build_program()
    in_maps = []
    for c in range(NCORE):
        in_maps.append({
            "x": np.ascontiguousarray(x[c * SPC:(c + 1) * SPC]),
            "qT": qT, "kT": kT, "vT": vT, "oT": oT,
            "gw": gw, "uw": uw, "dw": dw,
            "s_in": sin, "s_in_down": sind,
            "tri": tri, "e2": e2, "onesc": ones,
            "triT": triT, "negI": negI,
        })
    res = run_bass_kernel_spmd(nc, in_maps, core_ids=list(range(NCORE)))
    out = np.concatenate([np.asarray(r["y"]).reshape(SPC, S, D)
                          for r in res.results], axis=0)
    return out.reshape(B, S, D)


# version 28 (final)
# speedup vs baseline: 1.0333x; 1.0333x over previous
import contextlib
import numpy as np
import ml_dtypes

import concourse.bass as bass
import concourse.mybir as mybir
from concourse.alu_op_type import AluOpType as op
from concourse.bass_utils import run_bass_kernel_spmd

F32 = mybir.dt.float32
F16 = mybir.dt.float16
BF16 = mybir.dt.bfloat16
AF = mybir.ActivationFunctionType
AX = mybir.AxisListType.X

B, S, D, H, DK, I = 16, 2048, 1024, 16, 64, 4096
NCORE = 8
SPC = B // NCORE
TPC = SPC * S
ST = 512
NPASS = TPC // ST
NQ = S // 512


class KB:
    ENGINES = ("sync", "gpsimd", "tensor", "vector", "scalar")

    def __init__(self, nc):
        self.nc = nc
        self.streams = {e: [] for e in self.ENGINES}
        self.cnt = {}
        self.sems = {}

    def new_sem(self, es, name):
        self.sems[name] = es.enter_context(self.nc.semaphore(name))
        self.cnt[name] = 0

    SERIAL = {"vector": "V", "scalar": "Sc"}

    def emit(self, eng, fn, wait=(), inc=None, inc_by=1):
        sems = self.sems
        wait = tuple((s, v) for (s, v) in wait if v and v > 0)
        ser = self.SERIAL.get(eng)
        if ser is not None and inc is not None and inc != ser:
            ser = None
        elif ser is not None:
            assert inc is None or inc == ser
            inc, inc_by = ser, 1

        def run(e, fn=fn, wait=wait, inc=inc, inc_by=inc_by, ser=ser,
                k=(self.cnt[ser] if ser is not None else 0)):
            if ser is not None and k > 0:
                e.wait_ge(sems[ser], k)
            for s, v in wait:
                e.wait_ge(sems[s], v)
            r = fn(e)
            if inc is not None:
                r.then_inc(sems[inc], inc_by)

        self.streams[eng].append(run)
        if inc is not None:
            self.cnt[inc] += inc_by
            return self.cnt[inc]
        return None

    def flush(self, block):
        for eng in self.ENGINES:
            fns = self.streams[eng]
            if fns:
                getattr(block, eng)(lambda e, fns=tuple(fns): [f(e) for f in fns])
                self.streams[eng] = []

    def counts(self):
        return dict(self.cnt)


def emit_quant_apply(K, x_ap, rdq_ap, dq_ap, out_ap, tmp, wait=(), inc="V"):
    W = x_ap.shape[-1]
    nb = W // 16
    x3 = x_ap.rearrange("p (b k) -> p b k", k=16)
    rdq_b = rdq_ap.unsqueeze(2).broadcast_to([128, nb, 16])
    dq_b = dq_ap.unsqueeze(2).broadcast_to([128, nb, 16])
    A, Bt, C, Dt, E, Ft = (tmp[c] for c in "ABCDEF")
    A3 = A.rearrange("p (b k) -> p b k", k=16)
    v = K.emit
    sq = str(tmp["B"].dtype) == str(F32)
    v("vector", lambda e: e.tensor_tensor(A3, x3, rdq_b, op.mult), wait=wait)
    if sq:
        v("vector", lambda e: e.tensor_tensor(Bt, A, A, op.mult))
        thr = (0.0625, 0.5625, 1.5625, 3.0625, 6.25, 12.25, 25.0)
    else:
        v("vector", lambda e: e.tensor_single_scalar(Bt, A, -1.0, op.mult))
        v("vector", lambda e: e.tensor_tensor(Bt, A, Bt, op.max))
        thr = (0.25, 0.75, 1.25, 1.75, 2.5, 3.5, 5.0)
    v("vector", lambda e: e.tensor_scalar(C, A, 0.0, 2.0, op.is_ge, op.mult))
    v("vector", lambda e, t=thr[0]: e.tensor_single_scalar(Dt, Bt, t, op.is_ge))
    v("vector", lambda e, t=thr[1]: e.scalar_tensor_tensor(Dt, Bt, t, Dt, op.is_ge, op.add))
    v("vector", lambda e, t=thr[2]: e.scalar_tensor_tensor(Dt, Bt, t, Dt, op.is_ge, op.add))
    v("vector", lambda e, t=thr[3]: e.scalar_tensor_tensor(Dt, Bt, t, Dt, op.is_ge, op.add))
    v("vector", lambda e, t=thr[4]: e.tensor_single_scalar(E, Bt, t, op.is_ge))
    v("vector", lambda e, t=thr[5]: e.scalar_tensor_tensor(E, Bt, t, E, op.is_ge, op.add))
    v("vector", lambda e, t6=thr[6]: e.tensor_scalar(Ft, Bt, t6, 2.0, op.is_ge, op.mult))
    v("vector", lambda e: e.scalar_tensor_tensor(Dt, Dt, 0.5, E, op.mult, op.add))
    v("vector", lambda e: e.tensor_tensor(Dt, Dt, Ft, op.add))
    v("vector", lambda e, D3=Dt.rearrange("p (b k) -> p b k", k=16):
      e.tensor_tensor(D3, D3, dq_b, op.mult))
    return v("vector", lambda e:
             e.scalar_tensor_tensor(out_ap, C, 1.0, Dt, op.subtract, op.mult),
             inc=inc)


def emit_scale_pipeline(K, amax_ap, gs6_ap, gsinv_ap, dq_ap, rdq_ap, tmp,
                        wait=(), inc="V"):
    S0, T0, T1, T2, V0, G, GI = (tmp[c] for c in ("S0", "T0", "T1", "T2", "V0", "G", "GI"))
    v = K.emit
    v("vector", lambda e: e.tensor_scalar(S0, amax_ap, gs6_ap, 448.0, op.mult, op.min),
      wait=wait)
    v("vector", lambda e: e.tensor_single_scalar(S0, S0, 2.0 ** -6, op.max))
    v("vector", lambda e: e.memset(G, 2.0 ** -9))
    v("vector", lambda e: e.memset(GI, 2.0 ** 9))
    for k in (8, 4, 2, 1):
        thr = float(2.0 ** (k + 3))
        m2 = float(2.0 ** k - 1.0)
        m3 = float(2.0 ** -k - 1.0)
        v("vector", lambda e: e.tensor_tensor(T0, S0, GI, op.mult))
        v("vector", lambda e, thr=thr: e.tensor_single_scalar(T1, T0, thr, op.is_ge))
        v("vector", lambda e, m3=m3: e.tensor_scalar(T0, T1, m3, 1.0, op.mult, op.add))
        v("vector", lambda e: e.tensor_tensor(GI, GI, T0, op.mult))
        v("vector", lambda e, m2=m2: e.tensor_scalar(T0, T1, m2, 1.0, op.mult, op.add))
        v("vector", lambda e: e.tensor_tensor(G, G, T0, op.mult))
    v("vector", lambda e: e.tensor_tensor(V0, S0, GI, op.mult))
    v("vector", lambda e: e.tensor_single_scalar(T0, V0, 8.5, op.is_ge))
    for h in (9.5, 10.5, 11.5, 12.5, 13.5, 14.5, 15.5):
        v("vector", lambda e, h=h: e.scalar_tensor_tensor(T0, V0, h, T0,
                                                          op.is_ge, op.add))
    v("vector", lambda e: e.tensor_single_scalar(T0, T0, 8.0, op.add))
    v("vector", lambda e: e.tensor_tensor(S0, T0, G, op.mult))
    v("vector", lambda e: e.tensor_single_scalar(dq_ap, S0, gsinv_ap, op.mult))
    v("vector", lambda e: e.tensor_single_scalar(T0, dq_ap, 1e-3, op.max))
    v("vector", lambda e: e.reciprocal(T1, T0))
    return v("vector", lambda e: e.tensor_single_scalar(rdq_ap, T1, 1500.0, op.min),
             inc=inc)


def emit_ln_chunk(K, src_ap, out_ap, stat, sq_scratch, wait_load=(), inc="V",
                  apply_out=True):
    v = K.emit
    v("vector", lambda e: e.reduce_sum(stat[:, 0:1], src_ap, AX),
      wait=wait_load, inc="V")
    ssv = v("scalar", lambda e: e.activation(sq_scratch, src_ap, AF.Square,
                                             accum_out=stat[:, 1:2]),
            wait=wait_load, inc="Sc")

    v("vector", lambda e: e.tensor_single_scalar(stat[:, 0:1], stat[:, 0:1],
                                                 1.0 / 1024.0, op.mult),
      wait=[("Sc", ssv)])
    v("vector", lambda e: e.tensor_tensor(stat[:, 2:3], stat[:, 0:1],
                                          stat[:, 0:1], op.mult))
    v("vector", lambda e: e.scalar_tensor_tensor(stat[:, 2:3], stat[:, 1:2],
                                                 1.0 / 1024.0, stat[:, 2:3],
                                                 op.mult, op.subtract))
    v("vector", lambda e: e.tensor_single_scalar(stat[:, 2:3], stat[:, 2:3],
                                                 1e-5, op.add))
    rv = v("vector", lambda e: e.reciprocal(stat[:, 3:4], stat[:, 2:3]), inc="V")
    sqv = v("scalar", lambda e: e.activation(stat[:, 4:5], stat[:, 3:4], AF.Sqrt),
            wait=[("V", rv)], inc="Sc")
    v("vector", lambda e: e.tensor_tensor(stat[:, 5:6], stat[:, 4:5],
                                          stat[:, 4:5], op.mult),
      wait=[("Sc", sqv)])
    v("vector", lambda e: e.tensor_tensor(stat[:, 5:6], stat[:, 2:3],
                                          stat[:, 5:6], op.mult))
    v("vector", lambda e: e.tensor_scalar(stat[:, 5:6], stat[:, 5:6], -0.5, 1.5,
                                          op.mult, op.add))
    v("vector", lambda e: e.tensor_tensor(stat[:, 4:5], stat[:, 4:5],
                                          stat[:, 5:6], op.mult))
    if apply_out:
        return v("vector", lambda e: e.tensor_scalar(out_ap, src_ap, stat[:, 0:1],
                                                     stat[:, 4:5],
                                                     op.subtract, op.mult), inc=inc)
    v("vector", lambda e: e.tensor_copy(stat[:, 6:7], stat[:, 0:1]))
    return v("vector", lambda e: e.tensor_copy(stat[:, 7:8], stat[:, 4:5]), inc=inc)


def build_program():
    nc = bass.Bass("TRN2", target_bir_lowering=False, debug=False,
                   detect_race_conditions=False)
    es = contextlib.ExitStack()

    ap_x = nc.dram_tensor("x", [SPC, S, D], F32, kind="ExternalInput").ap()
    ap_qT = nc.dram_tensor("qT", [8, 128, 1024], F32, kind="ExternalInput").ap()
    ap_kT = nc.dram_tensor("kT", [8, 128, 1024], F32, kind="ExternalInput").ap()
    ap_vT = nc.dram_tensor("vT", [D, D], F32, kind="ExternalInput").ap()
    ap_oT = nc.dram_tensor("oT", [D, D], F32, kind="ExternalInput").ap()
    ap_gw = nc.dram_tensor("gw", [I, D], F32, kind="ExternalInput").ap()
    ap_uw = nc.dram_tensor("uw", [I, D], F32, kind="ExternalInput").ap()
    ap_dw = nc.dram_tensor("dw", [D, I], F32, kind="ExternalInput").ap()
    ap_sin = nc.dram_tensor("s_in", [1, 1], F32, kind="ExternalInput").ap()
    ap_sind = nc.dram_tensor("s_in_down", [1, 1], F32, kind="ExternalInput").ap()
    ap_tri = nc.dram_tensor("tri", [128, 128], BF16, kind="ExternalInput").ap()
    ap_triT = nc.dram_tensor("triT", [128, 128], BF16, kind="ExternalInput").ap()
    ap_negI = nc.dram_tensor("negI", [128, 128], BF16, kind="ExternalInput").ap()
    ap_e2 = nc.dram_tensor("e2", [128, 2], BF16, kind="ExternalInput").ap()
    ap_ones = nc.dram_tensor("onesc", [128, 128], BF16, kind="ExternalInput").ap()
    ap_y = nc.dram_tensor("y", [SPC, S, D], F32, kind="ExternalOutput").ap()

    import os as _os0
    _IKIND = "ExternalOutput" if _os0.environ.get("KDBG2") == "1" else "Internal"
    ap_gwq = nc.dram_tensor("gw_q", [I, D], BF16, kind=_IKIND).ap()
    ap_uwq = nc.dram_tensor("uw_q", [I, D], BF16, kind=_IKIND).ap()
    ap_dwq = nc.dram_tensor("dw_q", [D, I], BF16, kind=_IKIND).ap()
    ap_sscr = nc.dram_tensor("s_scr", [TPC, D], F16, kind=_IKIND).ap()
    ap_brel = nc.dram_tensor("b_rel", [SPC, 16], F32, kind="Internal").ap()
    ap_rrel = nc.dram_tensor("rl_rel", [2, 512], F32, kind="Internal").ap()
    ap_grel = nc.dram_tensor("g_rel", [4, 128], F32, kind="Internal").ap()
    ap_srel = nc.dram_tensor("s_rel", [1, 8], F32, kind="Internal").ap()
    ap_dxnf = nc.dram_tensor("d_xnf", [128, 8 * S], BF16, kind=_IKIND).ap()
    ap_dQ = nc.dram_tensor("d_Q", [128, 8 * S], BF16, kind=_IKIND).ap()
    ap_dnb = nc.dram_tensor("d_nb", [128, 16], F32, kind=_IKIND).ap()
    ap_dAV = nc.dram_tensor("d_AV", [128, 8 * S], BF16, kind=_IKIND).ap()
    import os as _os
    DBG = _os.environ.get("KDBG") == "1"
    if DBG:
        ap_dxq = nc.dram_tensor("d_xq", [ST, D], BF16, kind="Internal").ap()
        ap_dhid = nc.dram_tensor("d_hid", [ST, I], F16, kind="Internal").ap()
        ap_dhq = nc.dram_tensor("d_hq", [ST, I], BF16, kind="Internal").ap()
        ap_dg = nc.dram_tensor("d_g", [ST, D], F16, kind="Internal").ap()

    K = KB(nc)
    for sname in ("hD", "gD", "tD", "scD", "T", "V", "Sc"):
        K.new_sem(es, sname)

    def sb(stack, name, shape, dt):
        return stack.enter_context(nc.sbuf_tensor(name, shape, dt))

    t_triT = sb(es, "triT_sb", [128, 128], BF16)
    t_negI = sb(es, "negI_sb", [128, 128], BF16)
    t_e2 = sb(es, "e2_sb", [128, 2], BF16)
    t_ones = sb(es, "ones_sb", [128, 64], BF16)
    t_scal = sb(es, "scal_sb", [128, 8], F32)

    psum = [es.enter_context(nc.psum_tensor(f"pb{i}", [128, 512], F32))
            for i in range(8)]

    with nc.allow_low_precision(reason="fp16 quantizer / bf16 matmul pipeline"):
        blk = es.enter_context(nc.Block())

        for (dst, src) in ((t_e2[:], ap_e2), (t_ones[:], ap_ones[:, 0:64]),
                           (t_triT[:], ap_triT), (t_negI[:], ap_negI),
                           (t_scal[0:1, 0:1], ap_sin), (t_scal[0:1, 1:2], ap_sind)):
            K.emit("sync", lambda e, dst=dst, src=src: e.dma_start(out=dst, in_=src),
                   inc="hD", inc_by=16)
        consts_ld = K.cnt["hD"]

        K.emit("vector", lambda e: e.tensor_scalar(t_scal[0:1, 6:7], t_scal[0:1, 0:1],
                                                   0.0, 6.0, op.mult, op.add),
               wait=[("hD", consts_ld)])
        K.emit("vector", lambda e: e.reciprocal(t_scal[0:1, 7:8], t_scal[0:1, 6:7]))
        K.emit("vector", lambda e: e.tensor_single_scalar(t_scal[0:1, 2:3],
                                                          t_scal[0:1, 0:1],
                                                          t_scal[0:1, 7:8], op.mult))
        K.emit("vector", lambda e: e.tensor_single_scalar(t_scal[0:1, 3:4],
                                                          t_scal[0:1, 1:2],
                                                          t_scal[0:1, 7:8], op.mult))
        K.emit("vector", lambda e: e.reciprocal(t_scal[0:1, 4:5], t_scal[0:1, 0:1]))
        sprep = K.emit("vector", lambda e: e.reciprocal(t_scal[0:1, 5:6],
                                                        t_scal[0:1, 1:2]), inc="V")
        w1 = K.emit("sync", lambda e: e.dma_start(out=ap_srel, in_=t_scal[0:1, 0:8]),
                    wait=[("V", sprep)], inc="hD", inc_by=16)
        K.emit("sync", lambda e: e.dma_start(out=t_scal[:, 0:8],
                                             in_=ap_srel.broadcast_to([128, 8])),
               wait=[("hD", w1)], inc="hD", inc_by=16)
        scal_rdy = K.emit("vector",
                          lambda e: e.memset(t_scal[0:1, 6:7], 0.0),
                          wait=[("hD", K.cnt["hD"])], inc="V")
        GS6_A, GS6_D = t_scal[:, 2:3], t_scal[:, 3:4]
        GSI_A, GSI_D = t_scal[:, 4:5], t_scal[:, 5:6]

        wq_stack = contextlib.ExitStack()
        w_ld = [sb(wq_stack, f"wld{i}", [128, 1024], F32) for i in range(3)]
        w_amA = sb(wq_stack, "w_amA", [128, 2048], F32)
        w_dq = sb(wq_stack, "w_dq", [128, 512], F32)
        w_rdq = sb(wq_stack, "w_rdq", [128, 512], F32)
        w_gm = sb(wq_stack, "w_gm", [128, 4], F32)
        w_gs = sb(wq_stack, "w_gs", [128, 8], F32)
        w_rel = w_dq
        w_pool = {c: sb(wq_stack, f"wsc{c}", [128, 512], F32)
                  for c in ("S0", "T0", "T1", "V0", "G", "GI")}
        w_pool["T2"] = w_pool["G"]
        w_sc = w_pool
        w_qt = {k: w_pool[c] for k, c in
                zip("ABCDEF", ("S0", "T0", "T1", "T2", "V0", "S0"))}

        def gen_weight_units():
            wdefs = [(ap_gw, ap_gwq, [(t * 128, 0) for t in range(32)], 0),
                     (ap_uw, ap_uwq, [(t * 128, 0) for t in range(32)], 1),
                     (ap_dw, ap_dwq, [(t * 128, j * 1024) for t in range(8)
                                      for j in range(4)], 2)]
            NW = 3
            cast_done = [0] * NW
            store_done = [0] * NW
            for (aw, awq, tiles, widx) in wdefs:
                amv = 0
                for iu, (r0, c0) in enumerate(tiles):
                    buf = iu % NW
                    ld = K.emit("sync", lambda e, r0=r0, c0=c0, buf=buf, aw=aw:
                                e.dma_start(out=w_ld[buf][:],
                                            in_=aw[r0:r0 + 128, c0:c0 + 1024]),
                                wait=[("V", cast_done[buf]),
                                      ("hD", store_done[buf])],
                                inc="hD", inc_by=16)
                    amv = K.emit("vector", lambda e, iu=iu, buf=buf:
                                 e.tensor_reduce(
                                     w_amA[:, iu * 64:(iu + 1) * 64],
                                     w_ld[buf][:].rearrange("p (b k) -> p b k", k=16),
                                     AX, op.max, apply_absolute_value=True),
                                 wait=[("hD", ld)], inc="V")
                    cast_done[buf] = amv
                    if iu % 4 == 3:
                        yield None
                gmv = K.emit("vector", lambda e:
                             e.tensor_reduce(w_gm[:, 0:1], w_amA[:], AX, op.max),
                             wait=[("V", amv)], inc="V")
                wg1 = K.emit("sync", lambda e, widx=widx:
                             e.dma_start(out=ap_grel[widx, :].unsqueeze(0),
                                         in_=w_gm[:, 0:1]),
                             wait=[("V", gmv)], inc="hD", inc_by=16)
                ld2 = K.emit("sync", lambda e, widx=widx:
                             e.dma_start(out=w_rel[0:1, 0:128],
                                         in_=ap_grel[widx, :].unsqueeze(0)),
                             wait=[("hD", wg1)], inc="hD", inc_by=16)

                K.emit("vector", lambda e:
                       e.tensor_reduce(w_gm[0:1, 1:2], w_rel[0:1, 0:128], AX, op.max),
                       wait=[("hD", ld2)])
                K.emit("vector", lambda e: e.reciprocal(w_gm[0:1, 2:3], w_gm[0:1, 1:2]))
                K.emit("vector", lambda e: e.tensor_single_scalar(w_gm[0:1, 2:3],
                                                                  w_gm[0:1, 2:3],
                                                                  2688.0, op.mult))
                K.emit("vector", lambda e: e.tensor_single_scalar(w_gm[0:1, 3:4],
                                                                  w_gm[0:1, 2:3],
                                                                  t_scal[0:1, 7:8], op.mult))
                gsv = K.emit("vector", lambda e: e.reciprocal(w_gm[0:1, 0:1],
                                                              w_gm[0:1, 2:3]), inc="V")
                wg2 = K.emit("sync", lambda e: e.dma_start(out=ap_grel[3, 0:4].unsqueeze(0),
                                                           in_=w_gm[0:1, 0:4]),
                             wait=[("V", gsv)], inc="hD", inc_by=16)
                K.emit("sync", lambda e, widx=widx:
                       e.dma_start(out=w_gs[:, 2 * widx:2 * widx + 1],
                                   in_=ap_grel[3, 3:4].unsqueeze(0).broadcast_to([128, 1])),
                       wait=[("hD", wg2)], inc="hD", inc_by=16)
                gsb_h = K.emit("sync", lambda e, widx=widx:
                               e.dma_start(out=w_gs[:, 2 * widx + 1:2 * widx + 2],
                                           in_=ap_grel[3, 0:1].unsqueeze(0).broadcast_to([128, 1])),
                               inc="hD", inc_by=16)
                gsb = K.emit("vector", lambda e:
                             e.memset(w_gm[:, 2:3], 0.0),
                             wait=[("hD", gsb_h)], inc="V")
                yield None
                pend_st = []

                def flush_store():
                    buf2, r2, c2, qd2 = pend_st.pop(0)
                    store_done[buf2] = K.emit(
                        "sync", lambda e, r2=r2, c2=c2, buf2=buf2, awq=awq:
                        e.dma_start(out=awq[r2:r2 + 128, c2:c2 + 1024],
                                    in_=w_ld[buf2].bitcast(BF16)[:, 0:1024]),
                        wait=[("V", qd2)], inc="hD", inc_by=16)

                for g0 in range(0, len(tiles), 8):
                    grp = tiles[g0:g0 + 8]
                    scv = emit_scale_pipeline(
                        K, w_amA[:, g0 * 64:(g0 + len(grp)) * 64],
                        w_gs[:, 2 * widx:2 * widx + 1],
                        w_gs[:, 2 * widx + 1:2 * widx + 2],
                        w_dq[:, 0:len(grp) * 64], w_rdq[:, 0:len(grp) * 64],
                        {k: t[:, 0:len(grp) * 64] for k, t in w_sc.items()},
                        wait=[("V", gsb)])
                    yield None
                    for j, (r0, c0) in enumerate(grp):
                        buf = (g0 + j) % NW
                        if len(pend_st) >= NW - 1:
                            flush_store()
                        ld = K.emit("sync",
                                    lambda e, r0=r0, c0=c0, buf=buf, aw=aw:
                                    e.dma_start(out=w_ld[buf][:],
                                                in_=aw[r0:r0 + 128, c0:c0 + 1024]),
                                    wait=[("V", cast_done[buf]),
                                          ("hD", store_done[buf])],
                                    inc="hD", inc_by=16)
                        qd = 0
                        for hslice in range(2):
                            qd = emit_quant_apply(
                                K,
                                w_ld[buf][:, hslice * 512:(hslice + 1) * 512],
                                w_rdq[:, j * 64 + hslice * 32: j * 64 + (hslice + 1) * 32],
                                w_dq[:, j * 64 + hslice * 32: j * 64 + (hslice + 1) * 32],
                                w_ld[buf].bitcast(BF16)[:, hslice * 512:(hslice + 1) * 512],
                                {k: t[:] for k, t in w_qt.items()},
                                wait=([("V", scv), ("hD", ld)]
                                      if hslice == 0 else []))
                        cast_done[buf] = qd
                        pend_st.append((buf, r0, c0, qd))
                        yield None
                while pend_st:
                    flush_store()
            while True:
                yield None

        wq_gen = gen_weight_units()

        def wq_step(n=1):
            for _ in range(n):
                next(wq_gen)

        att = contextlib.ExitStack()
        a_xld = [sb(att, f"a_xld{i}", [128, D], F32) for i in range(2)]
        a_stat = sb(att, "a_stat", [128, 8], F32)
        a_xn = [sb(att, f"a_xn{i}", [128, D], BF16) for i in range(2)]
        a_wld = [sb(att, f"a_wld{i}", [128, 1024], BF16) for i in range(2)]
        a_sqw = sb(att, "a_sqw", [128, 1024], BF16)
        a_Q = sb(att, "a_Q", [128, 8 * S], BF16)
        a_Kt = sb(att, "a_K", [128, 8 * S], BF16)
        a_vw = sb(att, "a_vw", [128, 8 * 1024], BF16)
        a_V = sb(att, "a_V", [128, 16 * (H * 65)], BF16)
        a_PT = [sb(att, f"a_PT{i}", [128, 512], BF16) for i in range(4)]
        a_nrm = sb(att, "a_nrm", [128, 20], F32)
        a_nbb = sb(att, "a_nbb", [128, 16], F32)
        a_rlb = sb(att, "a_rlb", [128, 512], F32)
        a_stg = sb(att, "a_stg", [128, 512], BF16)
        a_so = [sb(att, "a_so0", [128, 512], F16)] * 2
        a_xnfm = sb(att, "a_xnfm", [128, 8 * S], BF16)
        a_AV = a_xnfm
        rlb_ap = {0: a_rlb[0:64, 0:512],
                  1: a_sqw.bitcast(F32)[0:64, 0:512]}

        seq_end = {s: 0 for s in K.cnt}
        for seq in range(SPC):
            seq_wait = list(seq_end.items()) if seq > 0 else []
            ln_apply = []
            tp_done = []
            for sc in range(16):
                buf = sc % 2
                pend = list(seq_wait) if sc < 2 else [("V", ln_apply[sc - 2])]
                ld = K.emit("sync", lambda e, seq=seq, sc=sc, buf=buf:
                            e.dma_start(out=a_xld[buf][:],
                                        in_=ap_x[seq, sc * 128:(sc + 1) * 128, :]),
                            wait=pend, inc="hD", inc_by=16)
                lw = [("hD", ld)]
                if sc >= 2:
                    lw.append(("tD", tp_done[sc - 2]))
                elif seq_wait:
                    lw += seq_wait
                xnv = emit_ln_chunk(K, a_xld[buf][:], a_xn[buf][:], a_stat,
                                    a_sqw[:, 0:1024], wait_load=lw)
                ln_apply.append(xnv)

                def _tp(e, sc=sc, buf=buf):
                    o3 = a_xnfm[:].rearrange("p (c s) -> p c s", s=S)[:, :, sc * 128:(sc + 1) * 128]
                    i3 = a_xn[buf][:].rearrange("p (c f) -> p c f", f=128)
                    return e.dma_start(out=o3, in_=i3, transpose=True)

                tpv = K.emit("scalar", _tp, wait=[("V", xnv)], inc="tD", inc_by=16)
                tp_done.append(tpv)
                if sc % 4 == 3:
                    wq_step()
            xn_rdy = tp_done[-1]
            if seq == 0 and _IKIND == "ExternalOutput":
                K.emit("sync", lambda e: e.dma_start(out=ap_dxnf, in_=a_xnfm[:]),
                       wait=[("tD", xn_rdy)], inc="hD", inc_by=16)

            qk_evac = []
            qk_mm_last = [0] * 16
            li = 0
            for (wap, dst, scale) in ((ap_qT, a_Q, 0.125), (ap_kT, a_Kt, 1.0)):
                for hkc in range(8):
                    buf = li % 2
                    pend = ([("T", qk_mm_last[li - 2])] if li >= 2 else list(seq_wait))
                    ld = K.emit("gpsimd", lambda e, wap=wap, hkc=hkc, buf=buf:
                                e.dma_start(out=a_wld[buf][:], in_=wap[hkc]),
                                wait=pend, inc="gD", inc_by=16)
                    mmv = None
                    for nq in range(NQ):
                        bank = (hkc * NQ + nq) % 2
                        pw = [("gD", ld), ("tD", xn_rdy)]
                        if len(qk_evac) >= 2:
                            pw.append(("Sc", qk_evac[-2]))
                        for dc in range(8):
                            mmv = K.emit(
                                "tensor",
                                lambda e, bank=bank, dc=dc, nq=nq, buf=buf:
                                e.matmul(psum[bank][:],
                                         a_wld[buf][:].rearrange("p (c n) -> p c n", n=128)[:, dc, :],
                                         a_xnfm[:, dc * S + nq * 512: dc * S + (nq + 1) * 512],
                                         start=(dc == 0), stop=(dc == 7)),
                                wait=(pw if dc == 0 else ()),
                                inc=("T" if dc == 7 else None))
                        ev = K.emit("scalar",
                                    lambda e, bank=bank, dst=dst, hkc=hkc, nq=nq, scale=scale:
                                    e.activation(dst[:, hkc * S + nq * 512: hkc * S + (nq + 1) * 512],
                                                 psum[bank][:], AF.Copy, scale=scale),
                                    wait=[("T", mmv)], inc="Sc")
                        qk_evac.append(ev)
                    qk_mm_last[li] = mmv
                    li += 1
                    if hkc % 4 == 3:
                        wq_step()
            qk_rdy = qk_evac[-1]
            if seq == 0 and _IKIND == "ExternalOutput":
                K.emit("sync", lambda e: e.dma_start(out=ap_dQ, in_=a_Q[:]),
                       wait=[("Sc", qk_rdy)], inc="hD", inc_by=16)

            vld = []
            for dc in range(8):
                ld = K.emit("gpsimd", lambda e, dc=dc:
                            e.dma_start(out=a_vw[:, dc * 1024:(dc + 1) * 1024],
                                        in_=ap_vT[dc * 128:(dc + 1) * 128, :]),
                            wait=(list(seq_wait) if dc == 0 else ()),
                            inc="gD", inc_by=16)
                vld.append(ld)
            vones = 0
            for h in range(H):
                col = h * 65 + 64
                vones = K.emit("vector", lambda e, col=col:
                               e.memset(a_V[:].rearrange("p (c n) -> p c n", n=H * 65)[:, :, col:col + 1], 1.0),
                               wait=(list(seq_wait) if h == 0 else ()), inc="V")
            vt_evac = []
            vt_mm_last = 0
            for sc in range(16):
                for nv in range(2):
                    bank = 2 + (sc * 2 + nv) % 2
                    pw = [("tD", xn_rdy), ("V", vones), ("gD", vld[-1])]
                    if len(vt_evac) >= 2:
                        pw.append(("Sc", vt_evac[-2]))
                    mmv = None
                    for dc in range(8):
                        mmv = K.emit(
                            "tensor",
                            lambda e, bank=bank, sc=sc, nv=nv, dc=dc:
                            e.matmul(psum[bank][:],
                                     a_xnfm[:, dc * S + sc * 128: dc * S + sc * 128 + 128],
                                     a_vw[:, dc * 1024 + nv * 512: dc * 1024 + (nv + 1) * 512],
                                     start=(dc == 0), stop=(dc == 7)),
                            wait=(pw if dc == 0 else ()),
                            inc=("T" if dc == 7 else None))
                    vt_mm_last = mmv

                    ev = None
                    for j in range(8):
                        off = sc * (H * 65) + (nv * 8 + j) * 65
                        ev = K.emit("scalar",
                                    lambda e, bank=bank, off=off, j=j:
                                    e.copy(a_V[:, off:off + 64],
                                           psum[bank][:, j * 64:(j + 1) * 64]),
                                    wait=([("T", mmv)] if j == 0 else ()),
                                    inc=("Sc" if j == 7 else None))
                    vt_evac.append(ev)
                if sc % 4 == 3:
                    wq_step()
            v_rdy = vt_evac[-1]

            nrm_last = 0
            for (qsrc, cb) in ((a_Q, 0), (a_Kt, 8)):
                for hc in range(8):
                    qmax = []
                    for hhf in range(2):
                        sqv = K.emit("scalar", lambda e, qsrc=qsrc, hc=hc, hhf=hhf:
                                     e.activation(a_sqw[:],
                                                  qsrc[:, hc * S + hhf * 1024:
                                                       hc * S + (hhf + 1) * 1024],
                                                  AF.Square),
                                     wait=[("Sc", v_rdy)] +
                                          ([("V", qmax[-1])] if qmax else
                                           ([("V", nrm_last)] if nrm_last else [])),
                                     inc="Sc")
                        for qq in range(2):
                            mmv = K.emit("tensor", lambda e, qq=qq:
                                         e.matmul(psum[6][0:2, 0:512], t_e2[:],
                                                  a_sqw[:, qq * 512:(qq + 1) * 512],
                                                  start=True, stop=True),
                                         wait=[("Sc", sqv)] + ([("V", qmax[-1])] if qmax else []),
                                         inc="T")
                            rv = K.emit("vector", lambda e, hhf=hhf, qq=qq:
                                        e.tensor_reduce(a_nrm[0:2, 16 + hhf * 2 + qq:
                                                              17 + hhf * 2 + qq],
                                                        psum[6][0:2, 0:512], AX, op.max),
                                        wait=[("T", mmv)], inc="V")
                            qmax.append(rv)
                    nrm_last = K.emit("vector", lambda e, cb=cb, hc=hc:
                                      e.tensor_reduce(a_nrm[0:2, cb + hc:cb + hc + 1],
                                                      a_nrm[0:2, 16:20], AX, op.max),
                                      inc="V")

            K.emit("vector", lambda e:
                   e.tensor_tensor(a_nrm[0:2, 0:8], a_nrm[0:2, 0:8],
                                   a_nrm[0:2, 8:16], op.mult),
                   wait=[("V", nrm_last)])
            bv = K.emit("vector", lambda e:
                        e.tensor_single_scalar(a_nrm[0:2, 0:8], a_nrm[0:2, 0:8],
                                               1e-12, op.max), inc="V")
            bsq = K.emit("scalar", lambda e: e.activation(a_nrm[0:2, 8:16],
                                                          a_nrm[0:2, 0:8], AF.Sqrt),
                         wait=[("V", bv)], inc="Sc")
            bneg = K.emit("vector", lambda e:
                          e.tensor_scalar(a_nrm[0:2, 8:16], a_nrm[0:2, 8:16],
                                          0.1, -1.0, op.add, op.mult),
                          wait=[("Sc", bsq)], inc="V")
            wb1 = K.emit("sync", lambda e, seq=seq:
                         e.dma_start(out=ap_brel[seq, :].unsqueeze(0),
                                     in_=a_nrm[0:2, 8:16]),
                         wait=[("V", bneg)], inc="hD", inc_by=16)
            bbh = K.emit("sync", lambda e, seq=seq:
                         e.dma_start(out=a_nbb[:],
                                     in_=ap_brel[seq, :].unsqueeze(0).broadcast_to([128, 16])),
                         wait=[("hD", wb1)], inc="hD", inc_by=16)
            nbb_rdy = K.emit("scalar", lambda e: e.memzero(a_nrm[:, 16:17]),
                             wait=[("hD", bbh)], inc="Sc")

            if seq == 0 and _IKIND == "ExternalOutput":
                K.emit("sync", lambda e: e.dma_start(out=ap_dnb, in_=a_nbb[:]),
                       wait=[("Sc", nbb_rdy)], inc="hD", inc_by=16)
            exp_bank = {}
            av_mm = {}
            bank_free = {}
            abank_of = {0: (4, 6), 1: (5, 7)}
            relay_st = {0: 0, 1: 0}
            ev_of = {0: 0, 1: 0}
            ev_last = 0
            shift_last = 0

            def emit_avepi1(hc, qq, avv):
                rds = {}
                for par in range(2):
                    abank = abank_of[par][qq % 2]
                    rw = [("T", avv[par])]
                    if relay_st[1 - par]:
                        rw.append(("gD", relay_st[1 - par]))
                    rcv = K.emit("vector", lambda e, abank=abank:
                                 e.reciprocal(a_rlb[64:65, 0:512],
                                              psum[abank][64:65, 0:512]),
                                 wait=rw, inc="V")
                    st = K.emit("gpsimd", lambda e, seq=seq, par=par:
                                e.dma_start(out=ap_rrel[par, :].unsqueeze(0),
                                            in_=a_rlb[64:65, 0:512]),
                                wait=[("V", rcv)], inc="gD", inc_by=16)
                    relay_st[par] = st
                    rw2 = [("gD", st)]
                    if ev_of[par]:
                        rw2.append(("V", ev_of[par]))
                    rds[par] = K.emit("gpsimd", lambda e, par=par:
                                      e.dma_start(out=rlb_ap[par],
                                                  in_=ap_rrel[par, :].unsqueeze(0)
                                                  .broadcast_to([64, 512])),
                                      wait=rw2, inc="gD", inc_by=16)
                return rds

            def emit_avepi2(hc, qq, rds):
                nonlocal ev_last, shift_last
                qq = qq
                evs = {}
                for par in range(2):
                    abank = abank_of[par][qq % 2]
                    if par == 0:
                        dst_ap = a_AV[0:64, hc * S + qq * 512: hc * S + (qq + 1) * 512]
                        evw = [("gD", rds[par])]
                    else:
                        dst_ap = a_stg[0:64, 0:512]
                        evw = [("gD", rds[par])] + ([("gD", shift_last)] if shift_last else [])
                    ev = K.emit(
                        "vector",
                        lambda e, abank=abank, dst_ap=dst_ap, par=par, qq=qq:
                        e.scalar_tensor_tensor(dst_ap, psum[abank][0:64, 0:512],
                                               1.0, rlb_ap[par],
                                               op.mult, op.mult),
                        wait=evw, inc="V")
                    evs[par] = ev
                    bank_free[(par, qq % 2)] = ev
                    ev_of[par] = ev
                    ev_last = ev
                shift_last = K.emit(
                    "gpsimd",
                    lambda e, hc=hc, qq=qq:
                    e.dma_start(out=a_AV[64:128, hc * S + qq * 512:
                                         hc * S + (qq + 1) * 512],
                                in_=a_stg[0:64, 0:512]),
                    wait=[("V", evs[1])], inc="gD", inc_by=16)

            pend1 = None
            pend2 = None
            for hc in range(8):
                for qq in range(NQ):
                    if pend2 is not None:
                        emit_avepi2(*pend2)
                        pend2 = None
                    if pend1 is not None:
                        rds = emit_avepi1(*pend1)
                        pend2 = (pend1[0], pend1[1], rds)
                        pend1 = None
                    nkc = 4 * qq + 4
                    avv = {}
                    for kc in range(nkc):
                        kbuf = kc % 2
                        pbuf = kc % 2
                        if qq >= 1 and kc == 2 * qq + 2:
                            wq_step(1)
                        r = 128 * kc - 512 * qq
                        r0 = max(r, 0)
                        qn = 512 - r0
                        smm = {}
                        for par in range(2):
                            key = (par, kbuf)
                            bank = (2 if par == 0 else 0) + kbuf
                            smw = [("Sc", nbb_rdy)]
                            if key in exp_bank:
                                smw.append(("Sc", exp_bank[key]))
                            mm = K.emit(
                                "tensor",
                                lambda e, bank=bank, hc=hc, par=par, kc=kc, qq=qq, r0=r0, qn=qn, r=r:
                                e.matmul(psum[bank][:, r0:r0 + qn],
                                         a_Kt[par * 64:par * 64 + 64,
                                              hc * S + kc * 128: hc * S + (kc + 1) * 128],
                                         a_Q[par * 64:par * 64 + 64,
                                             hc * S + qq * 512 + r0: hc * S + qq * 512 + r0 + qn],
                                         start=True, stop=(r < 0),
                                         tile_position=(par * 64, 0)),
                                wait=smw, inc=(None if r >= 0 else "T"))
                            if r >= 0:
                                mm = K.emit(
                                    "tensor",
                                    lambda e, bank=bank, r=r:
                                    e.matmul(psum[bank][:, r:r + 128],
                                             t_triT[:], t_negI[:],
                                             start=False, stop=True),
                                    inc="T")
                            smm[par] = mm
                        exs = {}
                        for par in range(2):
                            key = (par, pbuf)
                            pt = a_PT[2 * par + pbuf]
                            bank = (2 if par == 0 else 0) + kbuf
                            h = 2 * hc + par
                            jcol = (h % 2) * 8 + h // 2
                            ew = [("T", smm[par])]
                            if key in av_mm:
                                ew.append(("T", av_mm[key]))
                            exs[par] = K.emit(
                                "scalar",
                                lambda e, pt=pt, bank=bank, r0=r0, jcol=jcol:
                                e.activation(pt[:, r0:512],
                                             psum[bank][:, r0:512],
                                             AF.Exp,
                                             bias=a_nbb[:, jcol:jcol + 1]),
                                wait=ew, inc="Sc")
                            exp_bank[(par, kbuf)] = exs[par]
                        for par in range(2):
                            key = (par, pbuf)
                            pt = a_PT[2 * par + pbuf]
                            h = 2 * hc + par
                            abank = abank_of[par][qq % 2]
                            avw = [("Sc", exs[par])]
                            if kc == 0:
                                bk = bank_free.get((par, qq % 2), 0)
                                if bk:
                                    avw.append(("V", bk))
                            mm = K.emit(
                                "tensor",
                                lambda e, abank=abank, kc=kc, h=h, pt=pt, nkc=nkc, r0=r0:
                                e.matmul(psum[abank][0:65, r0:512],
                                         a_V[:, kc * (H * 65) + h * 65:
                                             kc * (H * 65) + h * 65 + 65],
                                         pt[:, r0:512],
                                         start=(kc == 0), stop=(kc == nkc - 1)),
                                wait=avw, inc="T")
                            av_mm[key] = mm
                            avv[par] = mm
                    pend1 = (hc, qq, avv)
                    wq_step(1)
            if pend2 is not None:
                emit_avepi2(*pend2)
            rds = emit_avepi1(*pend1)
            emit_avepi2(pend1[0], pend1[1], rds)
            pend1 = None
            pend2 = None
            av_rdy_h = shift_last
            av_rdy_v = ev_last

            old = []
            for pc in range(8):
                ld = K.emit("gpsimd", lambda e, pc=pc:
                            e.dma_start(out=a_vw[:, pc * 1024:(pc + 1) * 1024],
                                        in_=ap_oT[pc * 128:(pc + 1) * 128, :]),
                            wait=([("T", vt_mm_last)] if pc == 0 else ()),
                            inc="gD", inc_by=16)
                old.append(ld)
            so_ev = []
            so_dma = []
            for qc in range(16):
                for half in range(2):
                    buf = (qc * 2 + half) % 2
                    bank = 2 + (qc * 2 + half) % 2
                    pw = [("gD", av_rdy_h), ("V", av_rdy_v), ("gD", old[-1])]
                    if len(so_ev) >= 2:
                        pw.append(("Sc", so_ev[-2]))
                    if len(so_dma) >= 2:
                        pw.append(("scD", so_dma[-2]))
                    mmv = None
                    for pc in range(8):
                        mmv = K.emit(
                            "tensor",
                            lambda e, bank=bank, qc=qc, half=half, pc=pc:
                            e.matmul(psum[bank][:],
                                     a_AV[:, pc * S + qc * 128: pc * S + qc * 128 + 128],
                                     a_vw[:, pc * 1024 + half * 512: pc * 1024 + half * 512 + 512],
                                     start=(pc == 0), stop=(pc == 7)),
                            wait=(pw if pc == 0 else ()),
                            inc=("T" if pc == 7 else None))
                    ev = K.emit("scalar", lambda e, bank=bank, buf=buf:
                                e.copy(a_so[buf][:], psum[bank][:]),
                                wait=[("T", mmv)] +
                                     ([("scD", so_dma[-1])] if so_dma else []),
                                inc="Sc")
                    so_ev.append(ev)
                    dmv = K.emit("scalar", lambda e, seq=seq, qc=qc, half=half, buf=buf:
                                 e.dma_start(out=ap_sscr[seq * S + qc * 128:
                                                         seq * S + (qc + 1) * 128,
                                                         half * 512:(half + 1) * 512],
                                             in_=a_so[buf][:]),
                                 wait=[("Sc", so_ev[-1])], inc="scD", inc_by=16)
                    so_dma.append(dmv)
                if qc % 4 == 3:
                    wq_step()
            seq_end = K.counts()

        for _ in range(256):
            wq_step()
        att_end = K.counts()
        att.close()
        wq_stack.close()

        ffn = contextlib.ExitStack()
        f_stat = sb(ffn, "f_stat", [128, 16], F32)
        f_xam = sb(ffn, "f_xam", [128, 256], F32)
        f_xdq = sb(ffn, "f_xdq", [128, 256], F32)
        f_xrdq = sb(ffn, "f_xrdq", [128, 256], F32)
        f_xqT = sb(ffn, "f_xqT", [128, 8 * 512], BF16)
        f_gwT = [sb(ffn, f"f_gwT{i}", [128, 8 * 512], BF16) for i in range(2)]
        f_uwT = [sb(ffn, f"f_uwT{i}", [128, 8 * 512], BF16) for i in range(2)]
        f_hid = sb(ffn, "f_hid", [128, 4 * I], F16)
        f_ham = sb(ffn, "f_ham", [128, 1024], F16)
        f_hdq = sb(ffn, "f_hdq", [128, 1024], F16)
        f_hrdq = sb(ffn, "f_hrdq", [128, 1024], F16)
        f_hq = sb(ffn, "f_hq", [128, I], BF16)
        f_hqT = sb(ffn, "f_hqT", [128, 32 * 512], BF16)
        f_dwT = sb(ffn, "f_dwT", [128, 16 * 512], BF16)
        f_x = [sb(ffn, f"f_x{i}", [128, 4 * D], F32) for i in range(2)]
        f_s = sb(ffn, "f_s", [128, 4 * 512], F16)
        f_out = [sb(ffn, f"f_out{i}", [128, 512], F32) for i in range(2)]
        f_g = sb(ffn, "f_g", [128, 512], F16)
        f_p32 = {c: sb(ffn, f"fp32{c}", [128, 1024], F32)
                 for c in ("S0", "T0", "T1", "T2", "V0", "G", "GI")}
        f_qt32 = {k: f_p32[c] for k, c in
                  zip("ABCDEF", ("S0", "T0", "T1", "T2", "V0", "S0"))}
        f_qt = {k: f_p32[c].bitcast(F16) for k, c in
                zip("ABCDEF", ("S0", "T0", "T1", "T2", "V0", "S0"))}
        f_t1 = f_p32["GI"]
        import os as _os2
        DBG = _os2.environ.get("KDBG") == "1"

        PS = [dict() for _ in range(NPASS + 1)]
        final_out = 0
        out_hist = []

        def emit_loads(p):
            seq = (p * ST) // S
            row0 = (p * ST) % S
            fx = f_x[p % 2]
            if p == 0:
                w0 = list(att_end.items())
            elif p >= 2:
                w0 = [("V", PS[p - 2]["cv_last"])]
            else:
                w0 = []
            xls = []
            for sc in range(4):
                ld = K.emit("sync", lambda e, seq=seq, row0=row0, sc=sc, fx=fx:
                            e.dma_start(out=fx[:, sc * D:(sc + 1) * D],
                                        in_=ap_x[seq, row0 + sc * 128: row0 + (sc + 1) * 128, :]),
                            wait=(w0 if sc == 0 else ()), inc="hD", inc_by=16)
                xls.append(ld)
            PS[p]["xls"] = xls

        def emit_sload(p, half, sw):
            PS[p]["sld%d" % half] = K.emit(
                "sync", lambda e, p=p, half=half:
                e.dma_start(out=f_s[:].rearrange("p (c n) -> p c n", n=512),
                            in_=ap_sscr[p * ST:(p + 1) * ST,
                                        half * 512:(half + 1) * 512].rearrange("(c p) n -> p c n", p=128)),
                wait=sw, inc="hD", inc_by=16)

        def gen_preamble(p):
            fx = f_x[p % 2]
            xls = PS[p]["xls"]
            ln_v = 0
            for sc in range(4):
                emit_ln_chunk(K, fx[:, sc * D:(sc + 1) * D],
                              f_t1[:], f_stat, f_xqT[:, 0:1024],
                              wait_load=[("hD", xls[sc])],
                              apply_out=False)
                K.emit("vector", lambda e, sc=sc:
                       e.tensor_copy(f_stat[:, 8 + 2 * sc:10 + 2 * sc],
                                     f_stat[:, 6:8]),
                       inc="V")
                K.emit("vector", lambda e, sc=sc, fx=fx:
                       e.tensor_scalar(f_t1[:], fx[:, sc * D:(sc + 1) * D],
                                       f_stat[:, 8 + 2 * sc:9 + 2 * sc],
                                       f_stat[:, 9 + 2 * sc:10 + 2 * sc],
                                       op.subtract, op.mult),
                       inc="V")
                ln_v = K.emit("vector", lambda e, sc=sc:
                              e.tensor_reduce(
                                  f_xam[:, sc * 64:(sc + 1) * 64],
                                  f_t1[:].rearrange("p (b k) -> p b k", k=16),
                                  AX, op.max, apply_absolute_value=True),
                              inc="V")
                yield
            xsc = emit_scale_pipeline(K, f_xam[:], GS6_A, GSI_A, f_xdq[:], f_xrdq[:],
                                      {k: t[:, 0:256] for k, t in f_p32.items()},
                                      wait=[("V", ln_v), ("V", scal_rdy)])
            yield
            xq_tp = []
            PS[p]["xq_tp"] = xq_tp
            samm = PS[p - 1].get("samm_last", 0) if p >= 1 else 0
            for sc in range(4):
                pw = ([("V", xsc)] if sc == 0 else [])
                pw += ([("tD", xq_tp[-1])] if xq_tp
                       else ([("tD", PS[p - 1]["hqT_rdy"])] if p >= 1 else []))
                K.emit("vector", lambda e, sc=sc, fx=fx:
                       e.tensor_scalar(f_t1[:], fx[:, sc * D:(sc + 1) * D],
                                       f_stat[:, 8 + 2 * sc:9 + 2 * sc],
                                       f_stat[:, 9 + 2 * sc:10 + 2 * sc],
                                       op.subtract, op.mult),
                       wait=pw, inc="V")
                qd = emit_quant_apply(
                    K, f_t1[:],
                    f_xrdq[:, sc * 64:(sc + 1) * 64],
                    f_xdq[:, sc * 64:(sc + 1) * 64],
                    f_hq[:, 0:1024],
                    {k: t[:, 0:1024] for k, t in f_qt32.items()})

                def _tpq(e, sc=sc):
                    o3 = f_xqT[:].rearrange("p (c s) -> p c s", s=512)[:, :, sc * 128:(sc + 1) * 128]
                    i3 = f_hq[:, 0:1024].rearrange("p (c f) -> p c f", f=128)
                    return e.dma_start(out=o3, in_=i3, transpose=True)

                tw = [("V", qd)] + ([("T", samm)] if (sc == 0 and samm) else [])
                tv = K.emit("sync", _tpq, wait=tw, inc="tD", inc_by=16)
                xq_tp.append(tv)
                if DBG and p == 0:
                    K.emit("sync", lambda e, sc=sc:
                           e.dma_start(out=ap_dxq[sc * 128:(sc + 1) * 128, :],
                                       in_=f_hq[:, 0:1024]),
                           wait=[("V", qd)], inc="hD", inc_by=16)
                yield
            while True:
                yield

        def emit_stageA(p, pre):
            xqT_rdy = PS[p]["xq_tp"][-1]
            ga_prev = PS[p - 1]["ga67"] if p >= 1 else (0, 0)
            ga_ev = list(PS[p - 1]["ga_tail"]) if p >= 1 else []
            gu_mm_last = [0] * 8
            samm = 0
            for ic in range(8):
                buf = ic % 2
                if ic >= 2:
                    pend = [("T", gu_mm_last[ic - 2])]
                else:
                    pend = [("tD", xqT_rdy)]
                    if ga_prev[buf]:
                        pend.append(("T", ga_prev[buf]))
                    if ic == 0 and out_hist:
                        pend.append(("hD", out_hist[-1]))

                def _ldg(e, ic=ic, buf=buf):
                    i3 = ap_gwq[ic * 512:(ic + 1) * 512, :].rearrange("i (c p) -> i c p", p=128)
                    o3 = f_gwT[buf][:].rearrange("p (c n) -> p c n", n=512)
                    return e.dma_start(out=o3, in_=i3, transpose=True)

                def _ldu(e, ic=ic, buf=buf):
                    i3 = ap_uwq[ic * 512:(ic + 1) * 512, :].rearrange("i (c p) -> i c p", p=128)
                    o3 = f_uwT[buf][:].rearrange("p (c n) -> p c n", n=512)
                    return e.dma_start(out=o3, in_=i3, transpose=True)

                K.emit("sync", _ldg, wait=pend, inc="tD", inc_by=16)
                ld = K.emit("sync", _ldu, inc="tD", inc_by=16)
                for ss in range(4):
                    gb = (ss % 2) * 2
                    ub = gb + 1
                    pw = [("tD", ld)]
                    if ic == 0 and ss == 0 and ga_ev:
                        pw.append(("V", ga_ev[-1]))
                    elif len(ga_ev) >= 2:
                        pw.append(("V", ga_ev[-2]))
                    mmv = None
                    for hcc in range(8):
                        K.emit("tensor",
                               lambda e, gb=gb, ss=ss, hcc=hcc, buf=buf:
                               e.matmul(psum[gb][:],
                                        f_xqT[:, hcc * 512 + ss * 128: hcc * 512 + ss * 128 + 128],
                                        f_gwT[buf][:, hcc * 512:(hcc + 1) * 512],
                                        start=(hcc == 0), stop=(hcc == 7)),
                               wait=(pw if hcc == 0 else ()), inc=None)
                        mmv = K.emit("tensor",
                                     lambda e, ub=ub, ss=ss, hcc=hcc, buf=buf:
                                     e.matmul(psum[ub][:],
                                              f_xqT[:, hcc * 512 + ss * 128: hcc * 512 + ss * 128 + 128],
                                              f_uwT[buf][:, hcc * 512:(hcc + 1) * 512],
                                              start=(hcc == 0), stop=(hcc == 7)),
                                     inc=("T" if hcc == 7 else None))
                    gu_mm_last[ic] = mmv
                    samm = mmv
                    sv = K.emit("scalar", lambda e, gb=gb, ss=ss, ic=ic:
                                e.activation(
                                    f_hid[:, ss * I + ic * 512: ss * I + (ic + 1) * 512],
                                    psum[gb][:], AF.Silu),
                                wait=[("T", mmv)], inc="Sc")
                    hv = K.emit("vector", lambda e, ss=ss, ic=ic, ub=ub:
                                e.tensor_tensor(
                                    f_hid[:, ss * I + ic * 512: ss * I + (ic + 1) * 512],
                                    f_hid[:, ss * I + ic * 512: ss * I + (ic + 1) * 512],
                                    psum[ub][:], op.mult),
                                wait=[("Sc", sv)], inc="V")
                    ga_ev.append(hv)
                    if ic == 7:
                        emit_hq_unit(p, ss)
            PS[p]["ga67"] = (gu_mm_last[6], gu_mm_last[7])
            PS[p]["ga_tail"] = ga_ev[-2:]
            PS[p]["hid_rdy"] = ga_ev[-1]
            PS[p]["samm_last"] = samm

        def emit_hq_unit(p, sc):
            K.emit("vector", lambda e, sc=sc:
                   e.tensor_reduce(
                       f_ham[:, sc * 256:(sc + 1) * 256],
                       f_hid[:, sc * I:(sc + 1) * I].rearrange("p (b k) -> p b k", k=16),
                       AX, op.max, apply_absolute_value=True),
                   inc="V")
            if sc % 2 == 0:
                return
            pr = sc // 2
            emit_scale_pipeline(K, f_ham[:, pr * 512:(pr + 1) * 512], GS6_D, GSI_D,
                                f_hdq[:, pr * 512:(pr + 1) * 512],
                                f_hrdq[:, pr * 512:(pr + 1) * 512],
                                {k: t[:, 0:512] for k, t in f_p32.items()},
                                wait=([("V", scal_rdy)] if (p == 0 and pr == 0) else ()))
            hq_half = PS[p].setdefault("tp", {})
            dguard = PS[p - 1]["dmm_last"] if p >= 1 else 0
            tv = 0
            for qc in (sc - 1, sc):
                for j in range(2):
                    hw = []
                    if qc == 0 and j == 0:
                        hw.append(("tD", PS[p]["xq_tp"][-1]))
                    if (qc - 1, j) in hq_half:
                        hw.append(("tD", hq_half[(qc - 1, j)]))
                    qd = emit_quant_apply(
                        K, f_hid[:, qc * I + j * 2048: qc * I + (j + 1) * 2048],
                        f_hrdq[:, qc * 256 + j * 128: qc * 256 + (j + 1) * 128],
                        f_hdq[:, qc * 256 + j * 128: qc * 256 + (j + 1) * 128],
                        f_hq[:, j * 2048:(j + 1) * 2048],
                        {k: t[:] for k, t in f_qt.items()},
                        wait=hw)

                    def _tph(e, qc=qc, j=j):
                        o3 = f_hqT[:].rearrange("p (c s) -> p c s", s=512)[
                            :, 16 * j:16 * (j + 1), qc * 128:(qc + 1) * 128]
                        i3 = f_hq[:, j * 2048:(j + 1) * 2048].rearrange(
                            "p (c f) -> p c f", f=128)
                        return e.dma_start(out=o3, in_=i3, transpose=True)

                    tw = [("V", qd)] + ([("T", dguard)] if (qc == 0 and dguard) else [])
                    tv = K.emit("sync", _tph, wait=tw, inc="tD", inc_by=16)
                    hq_half[(qc, j)] = tv
            PS[p]["hqT_rdy"] = tv

        def emit_stageB(p, pre):
            seq = (p * ST) // S
            row0 = (p * ST) % S
            fx = f_x[p % 2]
            dguard = PS[p - 1]["dmm_last"] if p >= 1 else 0
            dmm_last = 0
            ft_last = 0
            cv_last = PS[p - 1]["cv_last"] if p >= 1 else 0
            ob = 0
            npre = {(0, 0): 3, (0, 1): 3, (1, 0): 3, (1, 1): 0}
            for half in range(2):
                if half == 1:
                    emit_sload(p, 1, [("V", ft_last)])
                sld = PS[p]["sld%d" % half]
                for iq in range(2):
                    def _ldd(e, half=half, iq=iq):
                        i3 = ap_dwq[half * 512:(half + 1) * 512,
                                    iq * 2048:(iq + 1) * 2048].rearrange("h (c p) -> h c p", p=128)
                        o3 = f_dwT[:].rearrange("p (c n) -> p c n", n=512)
                        return e.dma_start(out=o3, in_=i3, transpose=True)

                    if half + iq:
                        dw = [("T", dmm_last)]
                    else:
                        dw = [("T", dguard)] if dguard else []
                    dld = K.emit("sync", _ldd, wait=dw, inc="tD", inc_by=16)
                    for ss in range(4):
                        bank = 4 + ss
                        pw = [("tD", dld), ("tD", PS[p]["tp"][(ss, iq)]),
                              ("hD", sld)]
                        mmv = None
                        for ic in range(16):
                            icg = iq * 16 + ic
                            mmv = K.emit(
                                "tensor",
                                lambda e, bank=bank, ss=ss, icg=icg, ic=ic, iq=iq:
                                e.matmul(psum[bank][:],
                                         f_hqT[:, icg * 512 + ss * 128: icg * 512 + ss * 128 + 128],
                                         f_dwT[:, ic * 512:(ic + 1) * 512],
                                         start=(iq == 0 and ic == 0),
                                         stop=(iq == 1 and ic == 15)),
                                wait=(pw if ic == 0 else ()),
                                inc=("T" if ic == 15 else None))
                        dmm_last = mmv
                        if iq == 0:
                            continue
                        gv = K.emit("scalar", lambda e, bank=bank:
                                    e.activation(f_g[:], psum[bank][:], AF.Silu),
                                    wait=[("T", mmv)] +
                                         ([("V", cv_last)] if cv_last else []),
                                    inc="Sc")
                        if DBG and p == 0:
                            K.emit("sync", lambda e, ss=ss, half=half:
                                   e.dma_start(out=ap_dg[ss * 128:(ss + 1) * 128,
                                                         half * 512:(half + 1) * 512],
                                               in_=f_g[:]),
                                   wait=[("Sc", gv)], inc="hD", inc_by=16)
                        tvw = [("Sc", gv)]
                        if len(out_hist) >= 2:
                            tvw.append(("hD", out_hist[-2]))
                        tv = K.emit("vector", lambda e, ss=ss:
                                    e.tensor_tensor(
                                        f_g[:], f_g[:],
                                        f_s[:, ss * 512:(ss + 1) * 512],
                                        op.mult),
                                    wait=tvw, inc="V")
                        ft_last = tv
                        obuf = ob % 2
                        cv = K.emit("vector", lambda e, ss=ss, half=half, obuf=obuf, fx=fx:
                                    e.scalar_tensor_tensor(
                                        f_out[obuf][:], f_g[:], 1.0,
                                        fx[:, ss * D + half * 512: ss * D + half * 512 + 512],
                                        op.mult, op.add),
                                    inc="V")
                        cv_last = cv
                        od = K.emit("scalar",
                                    lambda e, seq=seq, row0=row0, ss=ss, half=half, obuf=obuf:
                                    e.dma_start(
                                        out=ap_y[seq, row0 + ss * 128: row0 + (ss + 1) * 128,
                                                 half * 512:(half + 1) * 512],
                                        in_=f_out[obuf][:]),
                                    wait=[("V", cv)], inc="hD", inc_by=16)
                        out_hist.append(od)
                        ob += 1
                    if pre is not None:
                        for _ in range(npre[(half, iq)]):
                            next(pre)
            PS[p]["dmm_last"] = dmm_last
            PS[p]["ft_last"] = ft_last
            PS[p]["cv_last"] = cv_last
            return out_hist[-1]

        emit_loads(0)
        emit_sload(0, 0, [("scD", att_end["scD"])])
        pre = gen_preamble(0)
        for _ in range(9):
            next(pre)
        for p in range(NPASS):
            if p + 1 < NPASS:
                emit_loads(p + 1)
            nxt = gen_preamble(p + 1) if p + 1 < NPASS else None
            emit_stageA(p, nxt)
            final_out = emit_stageB(p, nxt)
            if p + 1 < NPASS:
                emit_sload(p + 1, 0, [("V", PS[p]["ft_last"])])

        hT = K.cnt["hD"]
        gT = K.cnt["gD"]
        K.emit("sync", lambda e: None,
               wait=[("hD", hT), ("gD", gT),
                     ("scD", K.cnt["scD"]), ("tD", K.cnt["tD"])])

        K.flush(blk)
        ffn.close()
    es.close()
    return nc


_BF = ml_dtypes.bfloat16


def kernel(L0_attn_X, L0_attn_q_proj, L0_attn_k_proj, L0_attn_v_proj,
           L0_attn_o_proj, L0_ffn_gate_w, L0_ffn_up_w, L0_ffn_down_w,
           L0_ffn_s_in, L0_ffn_s_in_down):
    x = np.ascontiguousarray(np.asarray(L0_attn_X, dtype=np.float32))
    _qt = np.asarray(L0_attn_q_proj, dtype=np.float32).T
    _kt = np.asarray(L0_attn_k_proj, dtype=np.float32).T
    qT = np.ascontiguousarray(_qt.reshape(8, 128, 8, 128).transpose(2, 1, 0, 3)
                              .reshape(8, 128, 1024))
    kT = np.ascontiguousarray(_kt.reshape(8, 128, 8, 128).transpose(2, 1, 0, 3)
                              .reshape(8, 128, 1024))
    vT = np.ascontiguousarray(np.asarray(L0_attn_v_proj, dtype=np.float32).T)
    oT = np.ascontiguousarray(np.asarray(L0_attn_o_proj, dtype=np.float32).T)
    gw = np.ascontiguousarray(np.asarray(L0_ffn_gate_w, dtype=np.float32))
    uw = np.ascontiguousarray(np.asarray(L0_ffn_up_w, dtype=np.float32))
    dw = np.ascontiguousarray(np.asarray(L0_ffn_down_w, dtype=np.float32))
    sin = np.asarray(L0_ffn_s_in, dtype=np.float32).reshape(1, 1)
    sind = np.asarray(L0_ffn_s_in_down, dtype=np.float32).reshape(1, 1)
    tri = np.triu(np.ones((128, 128), dtype=np.float32)).astype(_BF)
    e2 = np.zeros((128, 2), dtype=np.float32)
    e2[0:64, 0] = 1.0
    e2[64:128, 1] = 1.0
    e2 = e2.astype(_BF)
    ones = np.ones((128, 128), dtype=np.float32).astype(_BF)
    triT = np.triu(np.ones((128, 128), dtype=np.float32), 1).astype(_BF)
    negI = (-30000.0 * np.eye(128, dtype=np.float32)).astype(_BF)

    nc = build_program()
    in_maps = []
    for c in range(NCORE):
        in_maps.append({
            "x": np.ascontiguousarray(x[c * SPC:(c + 1) * SPC]),
            "qT": qT, "kT": kT, "vT": vT, "oT": oT,
            "gw": gw, "uw": uw, "dw": dw,
            "s_in": sin, "s_in_down": sind,
            "tri": tri, "e2": e2, "onesc": ones,
            "triT": triT, "negI": negI,
        })
    res = run_bass_kernel_spmd(nc, in_maps, core_ids=list(range(NCORE)))
    out = np.concatenate([np.asarray(r["y"]).reshape(SPC, S, D)
                          for r in res.results], axis=0)
    return out.reshape(B, S, D)
